# version 02
# baseline (speedup 1.0000x reference)
import math
from contextlib import ExitStack

import numpy as np
import ml_dtypes
import concourse.bass as bass
import concourse.mybir as mybir
from concourse.bass_utils import run_bass_kernel_spmd

F32 = mybir.dt.float32
BF16 = mybir.dt.bfloat16
I32 = mybir.dt.int32
U32 = mybir.dt.uint32
AF = mybir.ActivationFunctionType
ALU = mybir.AluOpType
AX = mybir.AxisListType

ENGINES = ("pe", "dve", "act", "pool", "sp")


class _Op:
    __slots__ = ("eng", "fn", "reads", "writes", "scalar_reads", "dma",
                 "idx", "dma_idx", "edges", "marked", "milestone", "desc", "dsem")

    def __init__(self, eng, fn, reads, writes, scalar_reads, dma, desc):
        self.eng = eng
        self.fn = fn
        self.reads = reads
        self.writes = writes
        self.scalar_reads = scalar_reads
        self.dma = dma
        self.idx = -1
        self.dma_idx = -1
        self.dsem = 0
        self.edges = []
        self.marked = False
        self.milestone = 0
        self.desc = desc


def _conflict(k1, k2):
    if isinstance(k1, tuple):
        n1, t1 = k1[0], k1[1:]
    else:
        n1, t1 = k1, None
    if isinstance(k2, tuple):
        n2, t2 = k2[0], k2[1:]
    else:
        n2, t2 = k2, None
    if n1 != n2:
        return False
    return t1 is None or t2 is None or t1 == t2


class Prog:
    def __init__(self, nc):
        self.nc = nc
        self.ops = {e: [] for e in ENGINES}
        self.ndma = {e: 0 for e in ENGINES}
        self.last_writes = {}
        self.readers = {}
        self.barrier_tails = []

    def op(self, eng, fn, reads=(), writes=(), scalar_reads=(), dma=False,
           desc=""):
        ps_rd = tuple(k for k in reads
                      if (k[0] if isinstance(k, tuple) else k).startswith("ps"))
        o = _Op(eng, fn, tuple(reads) + tuple(scalar_reads),
                tuple(writes) + ps_rd, tuple(scalar_reads), dma, desc)
        o.idx = len(self.ops[eng])
        if dma:
            o.dma_idx = self.ndma[eng]
            self.ndma[eng] += 1
        self.ops[eng].append(o)

        def name_of(k):
            return k[0] if isinstance(k, tuple) else k

        deps = []
        for k in o.reads:
            for (wk, wop) in self.last_writes.get(name_of(k), ()):
                if _conflict(k, wk):
                    deps.append((wop, k in o.scalar_reads))
        for k in o.writes:
            for (wk, wop) in self.last_writes.get(name_of(k), ()):
                if _conflict(k, wk):
                    deps.append((wop, False))
            for (rk, rop) in self.readers.get(name_of(k), ()):
                if _conflict(k, rk):
                    deps.append((rop, False))
        o.edges.extend(self.barrier_tails)
        for (d, scalar) in deps:
            if d is o:
                continue
            if d.eng == eng and eng == "pe" and not d.dma and not dma:
                continue
            o.edges.append(d)

        for k in o.writes:
            n = name_of(k)
            lw = self.last_writes.setdefault(n, [])
            lw[:] = [(wk, wop) for (wk, wop) in lw if not _conflict(k, wk)]
            lw.append((k, o))
            rd = self.readers.setdefault(n, [])
            rd[:] = [(rk, rop) for (rk, rop) in rd if not _conflict(k, rk)]
        for k in o.reads:
            self.readers.setdefault(name_of(k), []).append((k, o))
        return o

    def barrier(self):
        self.barrier_tails = [self.ops[e][-1] for e in ENGINES if self.ops[e]]

    def dma(self, eng, out, in_, reads=(), writes=(), desc="", **kw):
        return self.op(eng, lambda E: E.dma_start(out=out, in_=in_, **kw),
                       reads=reads, writes=writes, dma=True, desc=desc)

    def finalize_and_emit(self, block, final_eng="sp"):
        tails = []
        for e in ENGINES:
            if self.ops[e]:
                tails.append(self.ops[e][-1])
        fence = self.op(final_eng, lambda E: None, desc="fence")
        fence.edges = [t for t in tails if t is not fence]

        for e in ENGINES:
            for o in self.ops[e]:
                if o.dma:
                    o.marked = True
                for d in o.edges:
                    d.marked = True
        NDSEM = 8
        sems = {}
        es = ExitStack()
        for e in ENGINES:
            cnt = 0
            dcnt = [0] * NDSEM
            di = 0
            for o in self.ops[e]:
                if o.marked:
                    if o.dma:
                        o.dsem = di % NDSEM
                        dcnt[o.dsem] += 1
                        o.milestone = dcnt[o.dsem] * 16
                        di += 1
                    else:
                        cnt += 1
                        o.milestone = cnt
            if any(o.marked and not o.dma for o in self.ops[e]):
                sems[(e, "c")] = es.enter_context(self.nc.semaphore(f"s_{e}_c"))
            for j in range(NDSEM):
                if any(o.marked and o.dma and o.dsem == j for o in self.ops[e]):
                    sems[(e, "d", j)] = es.enter_context(self.nc.semaphore(f"s_{e}_d{j}"))

        def emit_for(e, E):
            watermark = {}
            for o in self.ops[e]:
                need = {}
                for d in o.edges:
                    v = d.milestone
                    if v == 0:
                        continue
                    key = (d.eng, "d", d.dsem) if d.dma else (d.eng, "c")
                    if need.get(key, 0) < v:
                        need[key] = v
                if o.dma and o.milestone > 16:
                    key = (e, "d", o.dsem)
                    if need.get(key, 0) < o.milestone - 16:
                        need[key] = o.milestone - 16
                for key, v in sorted(need.items(), key=lambda kv: str(kv[0])):
                    if watermark.get(key, 0) >= v:
                        continue
                    E.wait_ge(sems[key], v)
                    watermark[key] = v
                try:
                    ins = o.fn(E)
                except Exception:
                    import sys as _sys
                    print(f"EMIT FAIL at {e}:{o.idx} desc={o.desc}", file=_sys.stderr)
                    raise
                if o.marked:
                    key = (e, "d", o.dsem) if o.dma else (e, "c")
                    assert ins is not None, f"marked op with no instruction: {o.desc}"
                    ins.then_inc(sems[key], 16 if o.dma else 1)

        emitters = {
            "pe": block.tensor, "dve": block.vector, "act": block.scalar,
            "pool": block.gpsimd, "sp": block.sync,
        }
        for e in ENGINES:
            if not self.ops[e]:
                continue
            def mk(fname, ee):
                def _f(E):
                    emit_for(ee, E)
                _f.__name__ = f"emit_{ee}"
                return _f
            emitters[e](mk(f"emit_{e}", e))
        return es


B, T, D = 4, 4096, 1024
NTOK = B * T
NCORES = 8
TLOC = NTOK // NCORES
NT = TLOC // 128
KD = D // 128


def core_batch(c):
    return c // 2, c % 2



class Ctx:

    def __init__(self, name):
        self.nc = bass.Bass("TRN2", target_bir_lowering=False)
        self.p = Prog(self.nc)
        self.es = ExitStack()
        self.tiles = {}
        self._block = None
        self.inputs = {}
        self.outputs = {}
        self.name = name

    def sb(self, name, shape, dtype):
        t = self.es.enter_context(self.nc.sbuf_tensor(name, shape, dtype))
        self.tiles[name] = t
        return t

    def ps(self, name, shape, dtype=F32):
        t = self.es.enter_context(self.nc.psum_tensor(name, shape, dtype))
        self.tiles[name] = t
        return t

    def inp(self, name, shape, dtype=F32):
        ap = self.nc.dram_tensor(name, list(shape), dtype, kind="ExternalInput").ap()
        self.inputs[name] = ap
        return ap

    def out(self, name, shape, dtype=F32):
        ap = self.nc.dram_tensor(name, list(shape), dtype, kind="ExternalOutput").ap()
        self.outputs[name] = ap
        return ap

    def dram(self, name, shape, dtype):
        return self.nc.dram_tensor(name, list(shape), dtype, kind="Internal").ap()

    def emit(self):
        block = self.es.enter_context(self.nc.Block())
        self.es.enter_context(self.p.finalize_and_emit(block))

    def close(self):
        self.es.close()


def make_identity(ctx, name="ident"):
    nc, p = ctx.nc, ctx.p
    ident = ctx.sb(name, [128, 128], BF16)
    p.op("pool", lambda E: E.memset(ident[:], 1.0), writes=[name], desc="id_set")
    p.op("pool", lambda E: E.affine_select(
        out=ident[:], in_=ident[:], compare_op=ALU.is_equal, fill=0.0,
        base=0, pattern=[[-1, 128]], channel_multiplier=1),
        reads=[name], writes=[name], desc="id_sel")
    return ident


def layernorm_tile(ctx, dst, src, scratch, stat, i, eps=1e-5,
                   dst_key=None, src_key=None, extra_scale=None, scratch_key=None):
    p = ctx.p
    sk = src_key or ("x", i)
    dk = dst_key or ("ln", i)
    st = ("lnstat", i)
    sck = scratch_key or ("lnscr", i)
    p.op("dve", lambda E: E.reduce_sum(stat[:, 0:1], src[:], axis=AX.X),
         reads=[sk], writes=[st], desc=f"ln_sum{i}")
    p.op("act", lambda E: E.activation(out=scratch[:], in_=src[:], func=AF.Square),
         reads=[sk], writes=[sck], desc=f"ln_sq{i}")
    p.op("dve", lambda E: E.reduce_sum(stat[:, 1:2], scratch[:], axis=AX.X),
         reads=[sck], writes=[st], desc=f"ln_sumsq{i}")
    p.op("dve", lambda E: E.tensor_scalar(stat[:, 0:1], stat[:, 0:1], 1.0 / D, None, ALU.mult),
         reads=[st], writes=[st], desc=f"ln_mean{i}")
    p.op("dve", lambda E: E.tensor_mul(stat[:, 2:3], stat[:, 0:1], stat[:, 0:1]),
         reads=[st], writes=[st], desc=f"ln_m2{i}")
    p.op("dve", lambda E: E.tensor_scalar(stat[:, 3:4], stat[:, 1:2], 1.0 / D, eps, ALU.mult, ALU.add),
         reads=[st], writes=[st], desc=f"ln_ssd{i}")
    p.op("dve", lambda E: E.tensor_sub(stat[:, 4:5], stat[:, 3:4], stat[:, 2:3]),
         reads=[st], writes=[st], desc=f"ln_var{i}")
    p.op("act", lambda E: E.activation(out=stat[:, 5:6], in_=stat[:, 4:5], func=AF.Sqrt),
         scalar_reads=[st], writes=[st], desc=f"ln_sqrt{i}")
    p.op("dve", lambda E: E.reciprocal(stat[:, 6:7], stat[:, 5:6]),
         reads=[st], writes=[st], desc=f"ln_rstd{i}")
    if extra_scale is not None:
        p.op("dve", lambda E: E.tensor_scalar(stat[:, 6:7], stat[:, 6:7], extra_scale, None, ALU.mult),
             reads=[st], writes=[st], desc=f"ln_es{i}")
    p.op("dve", lambda E: E.tensor_scalar(dst[:], src[:], stat[:, 0:1], stat[:, 6:7],
                                          ALU.subtract, ALU.mult),
         reads=[sk], scalar_reads=[st], writes=[dk], desc=f"ln_app{i}")


def transpose_128(ctx, psum_bank, bank_key, col, src_ap, ident, reads):
    p = ctx.p
    p.op("pe", lambda E: E.transpose(psum_bank[:, col * 128:(col + 1) * 128],
                                     src_ap, ident[:]),
         reads=list(reads) + ["ident"], writes=[bank_key], desc=f"tr{col}")


def load_weight_direct(ctx, sbname, hbm_ap, rows, cols, queue="pool"):
    assert rows % 128 == 0
    t = ctx.sb(sbname, [128, (rows // 128) * cols], BF16)
    src = hbm_ap.rearrange("(k p) n -> p k n", p=128)
    dst = t[:].rearrange("p (k n) -> p k n", n=cols)
    ctx.p.dma(queue, dst, src, writes=[sbname], desc=f"ldw_{sbname}")
    return t


def cast_weight_scratch(ctx, scrname, hbm_ap, rows, cols, queue="pool"):
    pr = (rows + 15) // 16 * 16
    scr = ctx.dram(scrname, [pr, cols], BF16)
    ctx.p.dma(queue, scr[0:rows, :], hbm_ap, writes=[scrname], desc=f"cast_{scrname}")
    return scr, pr


def load_weight_T(ctx, sbname, scr_ap, scrname, pr, rows, cols, queue="sp"):
    assert cols % 128 == 0 and pr % 16 == 0
    t = ctx.sb(sbname, [128, (cols // 128) * pr], BF16)
    for k in range(cols // 128):
        dst = t[:, k * pr:(k + 1) * pr]
        src = scr_ap[:, k * 128:(k + 1) * 128]
        ctx.p.dma(queue, dst, src, reads=[scrname], writes=[(sbname, k)],
                  desc=f"ldwT_{sbname}_{k}", transpose=True)
    return t, pr


def mm_chain(ctx, bank_ap, bank_key, pairs, desc):
    p = ctx.p
    n = len(pairs)
    for k, (lt, lk, rt, rk) in enumerate(pairs):
        p.op("pe", (lambda lt=lt, rt=rt, s=(k == 0), e=(k == n - 1), b=bank_ap:
                    (lambda E: E.matmul(b, lt, rt, start=s, stop=e)))(),
             reads=list(lk) + list(rk), writes=[bank_key], desc=f"{desc}_k{k}")


def build_L2_ffn(ctx, core, x_in, x_out, W, dbg=None):
    nc, p = ctx.nc, ctx.p
    ident = make_identity(ctx)
    G = 8

    xT = ctx.sb("xT", [128, G * D], BF16)
    yacc = ctx.sb("yacc", [128, G * D], F32)
    coef = ctx.sb("coef", [128, G * 16], F32)
    xtile = ctx.sb("xtile", [128, 2 * D], F32)
    lnt = ctx.sb("lnt", [128, 2 * D], BF16)
    lnscr = ctx.sb("lnscr", [128, D], F32)
    stat = ctx.sb("stat", [128, 8], F32)
    rp = ctx.sb("rp", [128, 16], F32)
    hbuf = ctx.sb("hbuf", [128, G * 1024], BF16)
    hTbuf = ctx.sb("hTbuf", [128, 1024], BF16)
    silu_s = ctx.sb("silu_s", [128, 512], F32)
    wg_st = ctx.sb("wg_st", [128, 2 * 8192], BF16)
    wu_st = ctx.sb("wu_st", [128, 2 * 8192], BF16)
    wd_st = ctx.sb("wd_st", [128, 2 * 8192], BF16)
    ps_tr = ctx.ps("ps_tr", [128, 1024], BF16)
    ps_g = ctx.ps("ps_g", [128, 512], F32)
    ps_u = ctx.ps("ps_u", [128, 512], F32)
    ps_d0 = ctx.ps("ps_d0", [128, 512], F32)
    ps_d1 = ctx.ps("ps_d1", [128, 512], F32)
    ps_r = ctx.ps("ps_r", [128, 8], F32)

    w_router = load_weight_direct(ctx, "w_router", W["router"], 1024, 8)

    x_in_t = x_in.rearrange("(i p) d -> i p d", p=128)
    x_out_t = x_out.rearrange("(i p) d -> i p d", p=128)

    def stream_in(buf, bufname, par, hbm_ap, rows, cols, tag):
        dst = buf[:, par * 8192: par * 8192 + (rows // 128) * cols]
        dst = dst.rearrange("p (k n) -> p k n", n=cols)
        src = hbm_ap.rearrange("(k p) n -> p k n", p=128)
        p.dma("pool", dst, src, writes=[(bufname, par)], desc=f"st_{bufname}_{tag}")
        return buf[:, par * 8192: par * 8192 + (rows // 128) * cols]

    wctr = 0
    for g in range(2):
        tiles = range(g * G, (g + 1) * G)

        for i in tiles:
            s = i - g * G
            db = i % 2
            xt = xtile[:, db * D:(db + 1) * D]
            lt = lnt[:, db * D:(db + 1) * D]
            p.dma("sp", xt, x_in_t[i], writes=[("x", db)], desc=f"ldx{i}")
            layernorm_tile(ctx, lt, xt, lnscr, stat, i, src_key=("x", db),
                           dst_key=("ln", db))
            for k in range(KD):
                transpose_128(ctx, ps_tr, "ps_tr", k, lt[:, k * 128:(k + 1) * 128],
                              ident, reads=[("ln", db)])
            dst = xT[:, s * D:(s + 1) * D]
            p.op("dve", lambda E, dst=dst: E.tensor_copy(dst, ps_tr[:]),
                 reads=["ps_tr"], writes=[("xT", s)], desc=f"evac_xT{i}")
            mm_chain(ctx, ps_r[:], "ps_r",
                     [(xT[:, (s * 8 + k) * 128:(s * 8 + k + 1) * 128], [("xT", s)],
                       w_router[:, k * 8:(k + 1) * 8], ["w_router"]) for k in range(8)],
                     desc=f"router{i}")
            p.op("act", lambda E: E.activation(out=rp[:, 0:8], in_=ps_r[:], func=AF.Exp),
                 reads=["ps_r"], writes=["rp"], desc=f"rexp{i}")
            p.op("dve", lambda E: E.reduce_sum(rp[:, 8:9], rp[:, 0:8], axis=AX.X),
                 reads=["rp"], writes=["rp"], desc=f"rsum{i}")
            p.op("dve", lambda E: E.reciprocal(rp[:, 9:10], rp[:, 8:9]),
                 reads=["rp"], writes=["rp"], desc=f"rrec{i}")
            p.op("dve", lambda E: E.tensor_scalar(rp[:, 0:8], rp[:, 0:8], rp[:, 9:10],
                                                  None, ALU.mult),
                 reads=["rp"], scalar_reads=["rp"], writes=["rp"], desc=f"rnorm{i}")
            p.op("dve", lambda E: E.max(out=rp[:, 8:16], in_=rp[:, 0:8]),
                 reads=["rp"], writes=["rp"], desc=f"rtop{i}")
            cf = coef[:, s * 16:s * 16 + 8]
            p.op("dve", lambda E, cf=cf: E.tensor_scalar(cf, rp[:, 0:8], rp[:, 9:10],
                                                         None, ALU.is_ge),
                 reads=["rp"], scalar_reads=["rp"], writes=[("coef", s)], desc=f"rmask{i}")
            p.op("dve", lambda E, cf=cf: E.tensor_mul(cf, cf, rp[:, 0:8]),
                 reads=["rp", ("coef", s)], writes=[("coef", s)], desc=f"rcoef{i}")

        for e in range(9):
            par = wctr % 2
            wctr += 1
            if e < 8:
                dint = 512
                wg = stream_in(wg_st, "wg_st", par, W["eg"][e], 1024, 512, f"{g}_{e}")
                wu = stream_in(wu_st, "wu_st", par, W["eu"][e], 1024, 512, f"{g}_{e}")
                wd = stream_in(wd_st, "wd_st", par, W["ed"][e], 512, 1024, f"{g}_{e}")
            else:
                dint = 1024
                wg = stream_in(wg_st, "wg_st", par, W["sg"], 1024, 1024, f"{g}_s")
                wu = stream_in(wu_st, "wu_st", par, W["su"], 1024, 1024, f"{g}_s")
                wd = stream_in(wd_st, "wd_st", par, W["sd"], 1024, 1024, f"{g}_s")
            wgk, wuk, wdk = ("wg_st", par), ("wu_st", par), ("wd_st", par)
            nh = dint // 128
            for i in tiles:
                s = i - g * G
                hreg = hbuf[:, s * 1024: s * 1024 + dint]
                hkey = ("h", s)
                for half in range((dint + 511) // 512):
                    w0 = half * 512
                    w1 = min(dint, w0 + 512)
                    gsl = ps_g[:, 0:w1 - w0]
                    usl = ps_u[:, 0:w1 - w0]
                    ssl = silu_s[:, 0:w1 - w0]
                    mm_chain(ctx, gsl, "ps_g",
                             [(xT[:, (s * 8 + k) * 128:(s * 8 + k + 1) * 128], [("xT", s)],
                               wg[:, k * dint + w0:k * dint + w1], [wgk]) for k in range(8)],
                             desc=f"g{g}_{e}_{i}_{half}")
                    p.op("act", lambda E, gsl=gsl, ssl=ssl: E.activation(
                        out=ssl, in_=gsl, func=AF.Silu),
                        reads=["ps_g"], writes=["silu_s"], desc=f"silu{e}_{i}_{half}")
                    mm_chain(ctx, usl, "ps_u",
                             [(xT[:, (s * 8 + k) * 128:(s * 8 + k + 1) * 128], [("xT", s)],
                               wu[:, k * dint + w0:k * dint + w1], [wuk]) for k in range(8)],
                             desc=f"u{g}_{e}_{i}_{half}")
                    hdst = hreg[:, w0:w1]
                    p.op("dve", lambda E, hdst=hdst, ssl=ssl, usl=usl: E.tensor_mul(
                        hdst, ssl, usl),
                        reads=["silu_s", "ps_u"], writes=[hkey], desc=f"h{e}_{i}_{half}")
                for k in range(nh):
                    transpose_128(ctx, ps_tr, "ps_tr", k, hreg[:, k * 128:(k + 1) * 128],
                                  ident, reads=[hkey])
                p.op("dve", lambda E, nh=nh: E.tensor_copy(hTbuf[:, 0:nh * 128],
                                                           ps_tr[:, 0:nh * 128]),
                     reads=["ps_tr"], writes=["hT"], desc=f"evac_hT{e}_{i}")
                for half in range(2):
                    bank, bkey = (ps_d0, "ps_d0") if half == 0 else (ps_d1, "ps_d1")
                    mm_chain(ctx, bank[:], bkey,
                             [(hTbuf[:, k * 128:(k + 1) * 128], ["hT"],
                               wd[:, k * 1024 + half * 512:k * 1024 + half * 512 + 512],
                               [wdk]) for k in range(nh)],
                             desc=f"d{g}_{e}_{i}_{half}")
                    ysl = yacc[:, s * D + half * 512:s * D + half * 512 + 512]
                    csl = coef[:, s * 16 + e:s * 16 + e + 1]
                    if e == 0:
                        p.op("dve", lambda E, ysl=ysl, csl=csl, bank=bank: E.tensor_scalar(
                            ysl, bank[:], csl, None, ALU.mult),
                            reads=[bkey], scalar_reads=[("coef", s)],
                            writes=[("yacc", s)], desc=f"acc{e}_{i}_{half}")
                    elif e < 8:
                        p.op("dve", lambda E, ysl=ysl, csl=csl, bank=bank: E.scalar_tensor_tensor(
                            ysl, bank[:], csl, ysl, ALU.mult, ALU.add),
                            reads=[bkey, ("yacc", s)], scalar_reads=[("coef", s)],
                            writes=[("yacc", s)], desc=f"acc{e}_{i}_{half}")
                    else:
                        p.op("dve", lambda E, ysl=ysl, bank=bank: E.tensor_add(
                            ysl, ysl, bank[:]),
                            reads=[bkey, ("yacc", s)],
                            writes=[("yacc", s)], desc=f"acc{e}_{i}_{half}")

        if dbg is not None and g == 0:
            p.dma("sp", dbg["xT"], xT[:], reads=[("xT", sl) for sl in range(G)], desc="dbg_xT")
            p.dma("sp", dbg["coef"], coef[:], reads=[("coef", sl) for sl in range(G)], desc="dbg_coef")
            p.dma("sp", dbg["yacc"], yacc[:], reads=[("yacc", sl) for sl in range(G)], desc="dbg_yacc")

        for i in tiles:
            s = i - g * G
            db = i % 2
            xt = xtile[:, db * D:(db + 1) * D]
            ysl = yacc[:, s * D:(s + 1) * D]
            p.dma("sp", xt, x_in_t[i], writes=[("x", db)], desc=f"ldx2_{i}")
            p.op("dve", lambda E, xt=xt, ysl=ysl: E.scalar_tensor_tensor(
                xt, ysl, 0.5, xt, ALU.mult, ALU.add),
                reads=[("yacc", s), ("x", db)], writes=[("x", db)], desc=f"res{i}")
            p.dma("sp", x_out_t[i], xt, reads=[("x", db)], writes=[("xo", i)],
                  desc=f"stx{i}")


def build_L2_attn(ctx, core, xb_in, xown_in, x_out, W):
    nc, p = ctx.nc, ctx.p
    ident = make_identity(ctx)
    TB = 4096
    NTB = TB // 128
    NP = 4
    scale = 1.0 / 8.0

    lnT = ctx.sb("lnT", [128, KD * TB], BF16)
    lnt = ctx.sb("lnt", [128, 2 * D], BF16)
    xtile = ctx.sb("xtile", [128, 2 * D], F32)
    lnscr = ctx.sb("lnscr", [128, D], F32)
    stat = ctx.sb("stat", [128, 8], F32)
    q2 = ctx.sb("q2", [128, TB], BF16)
    k2 = ctx.sb("k2", [128, TB], BF16)
    v2 = ctx.sb("v2", [128, NTB * 130], BF16)
    at = ctx.sb("at", [128, NP * TB], BF16)
    pt = ctx.sb("pt", [128, 2 * 512], BF16)
    masks = ctx.sb("masks", [128, 4 * 512], BF16)
    onesf = ctx.sb("onesf", [128, 128], F32)
    bvB = ctx.sb("bvB", [128, NP * 130], F32)
    bqc = ctx.sb("bqc", [128, NP], F32)
    bkc = ctx.sb("bkc", [128, NP], F32)
    rowbuf = ctx.sb("rowbuf", [128, 1536], F32)
    boB = ctx.sb("boB", [128, 1024], F32)
    bcS = ctx.sb("bcS", [128, 512], F32)
    ybuf = ctx.sb("ybuf", [128, 1024], F32)
    yout = ctx.sb("yout", [128, 1024], F32)
    wq_st = ctx.sb("wq_st", [128, 2 * 1024], BF16)
    wk_st = ctx.sb("wk_st", [128, 2 * 1024], BF16)
    wv_st = ctx.sb("wv_st", [128, 2 * 1024], BF16)
    rcp = ctx.sb("rcp", [128, 1024], F32)

    ps_tr = ctx.ps("ps_tr", [128, 1024], BF16)
    ps_a = ctx.ps("ps_a", [128, 512], F32)
    ps_b = ctx.ps("ps_b", [128, 512], F32)
    ps_s0 = ctx.ps("ps_s0", [128, 512], F32)
    ps_s1 = ctx.ps("ps_s1", [128, 512], F32)
    ps_at0 = ctx.ps("ps_at0", [128, 512], F32)
    ps_at1 = ctx.ps("ps_at1", [128, 512], F32)
    ps_bc = ctx.ps("ps_bc", [128, 512], F32)

    woscr, wopr = cast_weight_scratch(ctx, "woscr", W["wo"], 1024, 512)
    woT, _ = load_weight_T(ctx, "woT", woscr, "woscr", wopr, 1024, 512)

    p.op("pool", lambda E: E.memset(onesf[:], 1.0), writes=["onesf"], desc="ones")
    for m in range(4):
        dm = -m * 128
        msl = masks[:, m * 512:(m + 1) * 512]
        p.op("pool", lambda E, msl=msl: E.memset(msl, 1.0), writes=[("masks", m)], desc=f"mks{m}")
        p.op("pool", lambda E, msl=msl, dm=dm: E.affine_select(
            out=msl, in_=msl, compare_op=ALU.is_ge, fill=0.0,
            base=dm, pattern=[[1, 512]], channel_multiplier=-1),
            reads=[("masks", m)], writes=[("masks", m)], desc=f"mk{m}")
    for t in range(NTB):
        p.op("pool", lambda E, t=t: E.memset(v2[:, t * 130 + 64:t * 130 + 65], 1.0),
             writes=[("v2o", t)], desc=f"vone{t}")
        p.op("pool", lambda E, t=t: E.memset(v2[:, t * 130 + 129:t * 130 + 130], 1.0),
             writes=[("v2o", t)], desc=f"vone2{t}")
    p.dma("sp", bqc[:], W["bq"].rearrange("(c two) d -> (two d) c", two=2),
          writes=["bqc"], desc="bqc", allow_slow_non_contiguous=True)
    p.dma("sp", bkc[:], W["bk"].rearrange("(c two) d -> (two d) c", two=2),
          writes=["bkc"], desc="bkc", allow_slow_non_contiguous=True)
    p.dma("sp", rowbuf[0:1, 0:512], W["bv"].rearrange("h d -> (h d)")[None, :],
          writes=[("rowbuf", 0)], desc="bvrow")
    p.dma("sp", rowbuf[0:1, 512:1536], W["bo"][None, :], writes=[("rowbuf", 1)], desc="borow")
    p.op("pe", lambda E: E.matmul(ps_bc[:], onesf[0:1, :], rowbuf[0:1, 0:512],
                                  start=True, stop=True),
         reads=["onesf", ("rowbuf", 0)], writes=["ps_bc"], desc="bvB_mm")
    for pp in range(NP):
        for hh in range(2):
            dst = bvB[:, pp * 130 + hh * 65:pp * 130 + hh * 65 + 64]
            src = ps_bc[:, pp * 128 + hh * 64:pp * 128 + hh * 64 + 64]
            p.op("dve", lambda E, dst=dst, src=src: E.tensor_copy(dst, src),
                 reads=["ps_bc"], writes=["bvB"], desc=f"bvB{pp}_{hh}")
    for hh in range(2):
        p.op("pe", lambda E, hh=hh: E.matmul(ps_bc[:], onesf[0:1, :],
                                             rowbuf[0:1, 512 + hh * 512:512 + (hh + 1) * 512],
                                             start=True, stop=True),
             reads=["onesf", ("rowbuf", 1)], writes=["ps_bc"], desc=f"boB_mm{hh}")
        p.op("dve", lambda E, hh=hh: E.tensor_copy(boB[:, hh * 512:(hh + 1) * 512], ps_bc[:]),
             reads=["ps_bc"], writes=["boB"], desc=f"boB_e{hh}")

    xb_t = xb_in.rearrange("(i p) d -> i p d", p=128)
    xo_t = xown_in.rearrange("(i p) d -> i p d", p=128)
    x_out_t = x_out.rearrange("(i p) d -> i p d", p=128)
    yp_d = ctx.dram("yp_d", [TB, D], F32)
    ys_d = ctx.dram("ys_d", [TLOC, D], F32)
    yp_t = yp_d.rearrange("(i p) d -> i p d", p=128)
    ys_t = ys_d.rearrange("(i p) d -> i p d", p=128)

    for i in range(NTB):
        db = i % 2
        xt = xtile[:, db * D:(db + 1) * D]
        lt = lnt[:, db * D:(db + 1) * D]
        p.dma("sp", xt, xb_t[i], writes=[("x", db)], desc=f"ldx{i}")
        layernorm_tile(ctx, lt, xt, lnscr, stat, i, src_key=("x", db),
                       dst_key=("ln", db))
        for k in range(KD):
            transpose_128(ctx, ps_tr, "ps_tr", k, lt[:, k * 128:(k + 1) * 128],
                          ident, reads=[("ln", db)])
        for k in range(KD):
            dst = lnT[:, k * TB + i * 128:k * TB + (i + 1) * 128]
            p.op("dve", lambda E, dst=dst, k=k: E.tensor_copy(
                dst, ps_tr[:, k * 128:(k + 1) * 128]),
                reads=["ps_tr"], writes=[("lnT", i, k)], desc=f"evac_lnT{i}_{k}")

    def lnT_chunk(k, t0, n):
        return lnT[:, k * TB + t0:k * TB + t0 + n]

    def lnT_keys(k, t0, n):
        return [("lnT", t, k) for t in range(t0 // 128, (t0 + n) // 128)]

    for pp in range(NP):
        par = pp % 2
        for wst, wname, hb in ((wq_st, "wq_st", W["wq"]), (wk_st, "wk_st", W["wk"]),
                               (wv_st, "wv_st", W["wv"])):
            for hh in range(2):
                src = hb[2 * pp + hh].rearrange("(k p) d -> p k d", p=128)
                dst = wst[:, par * 1024:(par + 1) * 1024].rearrange(
                    "p (k t) -> p k t", t=128)[:, :, hh * 64:hh * 64 + 64]
                p.dma("pool", dst, src, writes=[(wname, par)], desc=f"ldw_{wname}{pp}_{hh}")
        wqs = wq_st[:, par * 1024:(par + 1) * 1024]
        wks = wk_st[:, par * 1024:(par + 1) * 1024]
        wvs = wv_st[:, par * 1024:(par + 1) * 1024]
        qk, kk, vk = ("wq_st", par), ("wk_st", par), ("wv_st", par)

        for dst_sb, dkey, wsl, wkey, bias, bkey, is_q in (
                (q2, "q2", wqs, qk, bqc, "bqc", True),
                (k2, "k2", wks, kk, bkc, "bkc", False)):
            for gqi in range(TB // 512):
                for hh in range(2):
                    bank, bkkey = (ps_a, "ps_a") if hh == 0 else (ps_b, "ps_b")
                    mm_chain(ctx, bank[hh * 64:hh * 64 + 64, :], bkkey,
                             [(wsl[:, k * 128 + hh * 64:k * 128 + hh * 64 + 64], [wkey],
                               lnT_chunk(k, gqi * 512, 512), lnT_keys(k, gqi * 512, 512))
                              for k in range(KD)],
                             desc=f"{dkey}{pp}_{gqi}_{hh}")
                    dst = dst_sb[hh * 64:hh * 64 + 64, gqi * 512:(gqi + 1) * 512]
                    bc = bias[hh * 64:hh * 64 + 64, pp:pp + 1]
                    if is_q:
                        p.op("dve", lambda E, dst=dst, bc=bc, bank=bank, hh=hh: E.tensor_scalar(
                            dst, bank[hh * 64:hh * 64 + 64, :], bc, scale, ALU.add, ALU.mult),
                            reads=[bkkey], scalar_reads=[bkey], writes=[(dkey, gqi)],
                            desc=f"ev{dkey}{pp}_{gqi}{hh}")
                    else:
                        p.op("dve", lambda E, dst=dst, bc=bc, bank=bank, hh=hh: E.tensor_scalar(
                            dst, bank[hh * 64:hh * 64 + 64, :], bc, None, ALU.add),
                            reads=[bkkey], scalar_reads=[bkey], writes=[(dkey, gqi)],
                            desc=f"ev{dkey}{pp}_{gqi}{hh}")
        for t in range(NTB):
            for hh in range(2):
                bank, bkkey = (ps_a, "ps_a") if hh == 0 else (ps_b, "ps_b")
                mm_chain(ctx, bank[:, 0:64], bkkey,
                         [(lnT_chunk(k, t * 128, 128), lnT_keys(k, t * 128, 128),
                           wvs[:, k * 128 + hh * 64:k * 128 + hh * 64 + 64], [vk])
                          for k in range(KD)],
                         desc=f"v{pp}_{t}_{hh}")
                dst = v2[:, t * 130 + hh * 65:t * 130 + hh * 65 + 64]
                bsl = bvB[:, pp * 130 + hh * 65:pp * 130 + hh * 65 + 64]
                p.op("dve", lambda E, dst=dst, bank=bank, bsl=bsl: E.tensor_add(
                    dst, bank[:, 0:64], bsl),
                    reads=[bkkey, "bvB"], writes=[("v2", t)], desc=f"evv{pp}_{t}_{hh}")

        for hh in range(2):
            hbase = hh * 64
            for gq in range(8):
                kcmax = (gq + 1) * 4
                abank, abkey = (ps_at0, "ps_at0") if hh == 0 else (ps_at1, "ps_at1")
                for kc in range(kcmax):
                    sbank, sbkey = (ps_s0, "ps_s0") if kc % 2 == 0 else (ps_s1, "ps_s1")
                    lt_ = k2[hbase:hbase + 64, kc * 128:(kc + 1) * 128]
                    rh_ = q2[hbase:hbase + 64, gq * 512:(gq + 1) * 512]
                    p.op("pe", lambda E, b=sbank, lt_=lt_, rh_=rh_: E.matmul(
                        b[:], lt_, rh_, start=True, stop=True),
                        reads=[("k2", kc // 4), ("q2", gq)], writes=[sbkey],
                        desc=f"sc{pp}{hh}_{gq}_{kc}")
                    ptb = pt[:, (kc % 2) * 512:(kc % 2) * 512 + 512]
                    ptk = ("pt", kc % 2)
                    dmask = gq * 512 - kc * 128
                    p.op("act", lambda E, b=sbank, ptb=ptb: E.activation(
                        out=ptb, in_=b[:], func=AF.Exp),
                        reads=[sbkey], writes=[ptk], desc=f"ex{pp}{hh}_{gq}_{kc}")
                    if dmask < 128:
                        m = (-dmask) // 128
                        msl = masks[:, m * 512:(m + 1) * 512]
                        p.op("dve", lambda E, ptb=ptb, msl=msl: E.tensor_mul(
                            ptb, ptb, msl),
                            reads=[ptk, ("masks", m)], writes=[ptk],
                            desc=f"mk{pp}{hh}_{gq}_{kc}")
                    vsl = v2[:, kc * 130 + hh * 65:kc * 130 + hh * 65 + 65]
                    p.op("pe", lambda E, a=abank, vsl=vsl, ptb=ptb,
                         s=(kc == 0), e=(kc == kcmax - 1): E.matmul(
                             a[0:65, :], vsl, ptb, start=s, stop=e),
                        reads=[("v2", kc), ("v2o", kc), ptk],
                        writes=[abkey], desc=f"pv{pp}{hh}_{gq}_{kc}")
                rsl = rcp[64:65, hh * 512:hh * 512 + 512]
                p.op("dve", lambda E, a=abank, rsl=rsl: E.reciprocal(rsl, a[64:65, :]),
                     reads=[abkey], writes=[("rcp", hh)], desc=f"rc{pp}{hh}_{gq}")
                p.op("pe", lambda E, rsl=rsl, hh=hh: E.matmul(
                    ps_bc[hh * 64:hh * 64 + 64, :], onesf[64:65, 0:64],
                    rsl, start=True, stop=True),
                    reads=[("rcp", hh), "onesf"], writes=["ps_bc"], desc=f"bc{pp}{hh}_{gq}")
                p.op("dve", lambda E, hh=hh: E.tensor_copy(
                    bcS[hh * 64:hh * 64 + 64, :], ps_bc[hh * 64:hh * 64 + 64, :]),
                    reads=["ps_bc"], writes=[("bcS", hh)], desc=f"bce{pp}{hh}_{gq}")
                dst = at[hbase:hbase + 64, pp * TB + gq * 512:pp * TB + (gq + 1) * 512]
                p.op("dve", lambda E, dst=dst, a=abank, hh=hh: E.tensor_mul(
                    dst, a[0:64, :], bcS[hh * 64:hh * 64 + 64, :]),
                    reads=[abkey, ("bcS", hh)], writes=[("at", pp, gq)],
                    desc=f"ate{pp}{hh}_{gq}")

    for t in range(NTB):
        gq = (t * 128) // 512
        for half in range(2):
            bank, bkey = (ps_a, "ps_a") if half == 0 else (ps_b, "ps_b")
            mm_chain(ctx, bank[:], bkey,
                     [(at[:, pp * TB + t * 128:pp * TB + (t + 1) * 128],
                       [("at", pp, gq)],
                       woT[:, pp * 1024 + half * 512:pp * 1024 + half * 512 + 512],
                       [("woT", pp)]) for pp in range(NP)],
                     desc=f"op{t}_{half}")
            p.op("dve", lambda E, bank=bank, half=half: E.tensor_copy(
                yout[:, half * 512:half * 512 + 512], bank[:]),
                reads=[bkey], writes=[("yout", half)], desc=f"ope{t}_{half}")
        p.dma("sp", yp_t[t], yout[:], reads=[("yout", 0), ("yout", 1)],
              writes=[("yp", t)], desc=f"styp{t}")

    groups = [[2 * g, 2 * g + 1] for g in range(4)]
    p.op("pool", lambda E: E.collective_compute(
        "ReduceScatter", ALU.add, groups, [yp_d.opt()], [ys_d.opt()]),
        reads=[("yp", t) for t in range(NTB)], writes=["ys"], desc="rs")

    for i in range(NT):
        db = i % 2
        xt = xtile[:, db * D:(db + 1) * D]
        p.dma("sp", xt, xo_t[i], writes=[("x", db)], desc=f"ldxo{i}")
        p.dma("sp", ybuf[:], ys_t[i], reads=["ys"], writes=["ybuf"], desc=f"ldys{i}")
        p.op("dve", lambda E, xt=xt: E.tensor_add(xt, xt, ybuf[:]),
             reads=["ybuf", ("x", db)], writes=[("x", db)], desc=f"r1{i}")
        p.op("dve", lambda E, xt=xt: E.tensor_add(xt, xt, boB[:]),
             reads=["boB", ("x", db)], writes=[("x", db)], desc=f"r2{i}")
        p.dma("sp", x_out_t[i], xt, reads=[("x", db)], writes=[("xo", i)],
              desc=f"stx{i}")


def build_L0_attn(ctx, core, xb_in, xown_in, x_out, W, idxq_in, hcol_in, phases=99):
    nc, p = ctx.nc, ctx.p
    ident = make_identity(ctx)
    TB = 4096
    NTB = 32
    EPS = 1e-5

    big1 = ctx.sb("big1", [128, 32800], BF16)
    big2 = ctx.sb("big2", [128, 24576], BF16)
    big3 = ctx.sb("big3", [128, 16384], BF16)
    big4 = ctx.sb("big4", [128, 16384], BF16)
    chunkscr = ctx.sb("chunkscr", [128, 4096], F32)
    lnt = ctx.sb("lnt", [128, D], BF16)
    xtile = ctx.sb("xtile", [128, D], F32)
    lnscr = None
    stat = ctx.sb("stat", [128, 8], F32)
    stats = ctx.sb("stats", [128, 64], F32)
    sco = ctx.sb("sco", [128, 64], F32)
    gb = ctx.sb("gb", [128, 48], F32)
    qtb = ctx.sb("qtb", [128, 1024], BF16)
    kst = ctx.sb("kst", [128, 2 * 1024], BF16)
    idxs = ctx.sb("idxs", [128, 16], I32)
    hcol = ctx.sb("hcol", [128, 1], F32)
    iotac = ctx.sb("iotac", [128, 4], F32)
    qpf = ctx.sb("qpf", [128, 256], BF16)
    onesc = ctx.sb("onesc", [128, 4], BF16)
    rcol = ctx.sb("rcol", [128, 4], F32)
    escr = ctx.sb("escr", [128, 256], F32)
    bsb = big1[:, 0:3072]
    pwrow = big1[:, 3072:6144]
    ps = [ctx.ps(f"psb{i}", [128, 512], F32) for i in range(7)]
    ps_tr = ctx.ps("ps_tr", [128, 1024], BF16)

    xb_t = xb_in.rearrange("(i p) d -> i p d", p=128)
    xo_t = xown_in.rearrange("(i p) d -> i p d", p=128)
    x_out_t = x_out.rearrange("(i p) d -> i p d", p=128)

    p.op("pool", lambda E: E.iota(iotac[:, 0:1], pattern=[[0, 1]], base=0,
                                  channel_multiplier=1,
                                  allow_small_or_imprecise_dtypes=True),
         writes=[("iotac", 0)], desc="iota")
    p.op("pool", lambda E: E.iota(qpf[:], pattern=[[1, 256]], base=0,
                                  channel_multiplier=0,
                                  allow_small_or_imprecise_dtypes=True),
         writes=["qpf"], desc="qp")
    p.op("pool", lambda E: E.memset(onesc[:], 1.0), writes=["onesc"], desc="onesc")
    p.dma("sp", hcol[:], hcol_in, writes=["hcol"], desc="hcol")
    p.dma("sp", idxs[:], idxq_in.rearrange("(t p) one -> p (t one)", p=128),
          writes=["idxs"], desc="idx", allow_slow_non_contiguous=True)
    p.op("dve", lambda E: E.tensor_sub(iotac[:, 1:2], iotac[:, 0:1], hcol[:]),
         reads=[("iotac", 0), "hcol"], writes=[("iotac", 1)], desc="hbase")
    p.dma("sp", gb[:, 0:24], W["gamma"].rearrange("(c p) -> p c", p=128),
          writes=["gb"], desc="gam", allow_slow_non_contiguous=True)
    p.dma("sp", gb[:, 24:48], W["beta"].rearrange("(c p) -> p c", p=128),
          writes=["gb"], desc="bet", allow_slow_non_contiguous=True)

    ascr = ctx.dram("ascr", [1024, 128], BF16)
    p.dma("pool", ascr[:, 0:64], W["loraA"], writes=["ascr"], desc="ascr")
    a2scr = ctx.dram("a2scr", [1024, 128], BF16)
    p.dma("pool", a2scr[:, 0:64], W["loraA2"], writes=["a2scr"], desc="a2scr")
    zpad = ctx.sb("zpad", [128, 512], BF16)
    p.op("pool", lambda E: E.memset(zpad[:], 0.0), writes=["zpad"], desc="zpad")
    for scr, nm in ((ascr, "ascr"), (a2scr, "a2scr")):
        dstz = scr.rearrange("(k p) c -> p k c", p=128)[:, :, 64:128]
        srcz = zpad[:].rearrange("p (k c) -> p k c", c=64)
        p.dma("pool", dstz, srcz, reads=["zpad"], writes=[nm], desc=f"z_{nm}")
    at_sb = big1[:, 6144:8192]
    for k in range(8):
        p.dma("sp", at_sb[:, k * 128:(k + 1) * 128],
              ascr[k * 128:(k + 1) * 128, :], reads=["ascr"],
              writes=[("at_sb", 0)], desc=f"atT{k}", transpose=True)
        p.dma("sp", at_sb[:, 1024 + k * 128:1024 + (k + 1) * 128],
              a2scr[k * 128:(k + 1) * 128, :], reads=["a2scr"],
              writes=[("at_sb", 1)], desc=f"a2T{k}", transpose=True)
    p.dma("pool", bsb[0:64, :], W["loraB"], writes=[("bsb", 0)], desc="ldB")
    wm_d = ctx.dram("wm_d", [1024, 3072], BF16)
    wv_ap = W["wattn"].rearrange("(a b) c -> a (b c)", b=3)
    for i in range(8):
        for j in range(3):
            bank = ps[j % 2]
            mm_chain(ctx, bank[:], f"psb{j % 2}",
                     [(at_sb[0:64, i * 128:(i + 1) * 128], [("at_sb", 0)],
                       bsb[0:64, j * 1024:j * 1024 + 512], [("bsb", 0)])],
                     desc=f"pA{i}_{j}a")
            mm_chain(ctx, ps[2 + j % 2][:], f"psb{2 + j % 2}",
                     [(at_sb[0:64, i * 128:(i + 1) * 128], [("at_sb", 0)],
                       bsb[0:64, j * 1024 + 512:(j + 1) * 1024], [("bsb", 0)])],
                     desc=f"pA{i}_{j}b")
        p.dma("pool", pwrow[:], wv_ap[i * 128:(i + 1) * 128, :], writes=["pwrow"],
              desc=f"wrow{i}")
        for j in range(3):
            for hf in range(2):
                bank = ps[j % 2] if hf == 0 else ps[2 + j % 2]
                bkey = f"psb{j % 2}" if hf == 0 else f"psb{2 + j % 2}"
                dst = chunkscr[:, j * 1024 + hf * 512:j * 1024 + hf * 512 + 512]
                src_w = pwrow[:, j * 1024 + hf * 512:j * 1024 + hf * 512 + 512]
                p.op("dve", lambda E, dst=dst, bank=bank, src_w=src_w: E.tensor_add(
                    dst, bank[:], src_w),
                    reads=[bkey, "pwrow"], writes=["chunkscr"], desc=f"wm{i}_{j}_{hf}")
        p.dma("pool", wm_d[i * 128:(i + 1) * 128, :], chunkscr[:, 0:3072],
              reads=["chunkscr"], writes=["wm_d"], desc=f"stwm{i}")
    p.dma("pool", bsb[0:64, 0:1024], W["loraB2"], writes=[("bsb", 0)], desc="ldB2")
    wmp_d = ctx.dram("wmp_d", [1024, 1024], BF16)
    for i in range(8):
        for hf in range(2):
            bank = ps[hf]
            mm_chain(ctx, bank[:], f"psb{hf}",
                     [(at_sb[0:64, 1024 + i * 128:1024 + (i + 1) * 128], [("at_sb", 1)],
                       bsb[0:64, hf * 512:hf * 512 + 512], [("bsb", 0)])],
                     desc=f"p2_{i}_{hf}")
        p.dma("pool", pwrow[:, 0:1024], W["wproj"][i * 128:(i + 1) * 128, :],
              writes=["pwrow"], desc=f"wprow{i}")
        for hf in range(2):
            dst = chunkscr[:, hf * 512:hf * 512 + 512]
            p.op("dve", lambda E, dst=dst, hf=hf: E.tensor_add(
                dst, ps[hf][:], pwrow[:, hf * 512:hf * 512 + 512]),
                reads=[f"psb{hf}", "pwrow"], writes=["chunkscr"], desc=f"wmp{i}_{hf}")
        p.dma("pool", wmp_d[i * 128:(i + 1) * 128, :], chunkscr[:, 0:1024],
              reads=["chunkscr"], writes=["wmp_d"], desc=f"stwmp{i}")
    wpscr, _ = cast_weight_scratch(ctx, "wpscr", W["wproj"], 1024, 1024)
    p.barrier()
    if phases <= 1:
        return

    wm_view = wm_d.rearrange("a (b c) -> (a b) c", b=3)
    for k in range(8):
        dst = big2[:, k * 3072:(k + 1) * 3072]
        p.dma("sp", dst, wm_view[:, k * 128:(k + 1) * 128], reads=["wm_d"],
              writes=[("WmT", k)], desc=f"ldWmT{k}", transpose=True)

    for i in range(NTB):
        p.dma("sp", xtile[:], xb_t[i], writes=[("x", 0)], desc=f"ldx{i}")
        layernorm_tile(ctx, lnt[:], xtile[:], chunkscr[:, 0:1024], stat, i,
                       src_key=("x", 0), dst_key=("ln", 0), scratch_key="chunkscr")
        for k in range(KD):
            transpose_128(ctx, ps_tr, "ps_tr", k, lnt[:, k * 128:(k + 1) * 128],
                          ident, reads=[("ln", 0)])
        for k in range(KD):
            dst = big1[:, k * TB + i * 128:k * TB + (i + 1) * 128]
            p.op("dve", lambda E, dst=dst, k=k: E.tensor_copy(
                dst, ps_tr[:, k * 128:(k + 1) * 128]),
                reads=["ps_tr"], writes=[("lnT", i, k)], desc=f"elnT{i}_{k}")

    def lnT_chunk(k, t0, n):
        return big1[:, k * TB + t0:k * TB + t0 + n]

    def lnT_keys(k, t0, n):
        return [("lnT", t, k) for t in range(t0 // 128, (t0 + n) // 128)]

    xa_d = ctx.dram("xa_d", [16 * 128, TB], BF16)
    for c in range(24):
        for n in range(8):
            bank, bkey = ps[n % 2], f"psb{n % 2}"
            mm_chain(ctx, bank[:], bkey,
                     [(big2[0:128, k * 3072 + c * 128:k * 3072 + (c + 1) * 128],
                       [("WmT", k)],
                       lnT_chunk(k, n * 512, 512), lnT_keys(k, n * 512, 512))
                      for k in range(KD)],
                     desc=f"xa{c}_{n}")
            dst = chunkscr[:, n * 512:(n + 1) * 512]
            p.op("dve", lambda E, dst=dst, bank=bank: E.tensor_copy(dst, bank[:]),
                 reads=[bkey], writes=[("chunkscr", n)], desc=f"exa{c}_{n}")
            sqb, sqk = ps[4], "psb4"
            p.op("act", lambda E, dst=dst, sqb=sqb: E.activation(
                out=sqb[:], in_=dst, func=AF.Square),
                reads=[("chunkscr", n)], writes=[sqk], desc=f"sq{c}_{n}")
            p.op("dve", lambda E, sqb=sqb: E.reduce_sum(stats[:, 49:50], sqb[:], axis=AX.X),
                 reads=[sqk], writes=[("stats", "t")], desc=f"sqr{c}_{n}")
            if n == 0:
                p.op("dve", lambda E, c=c: E.tensor_copy(stats[:, 24 + c:25 + c], stats[:, 49:50]),
                     reads=[("stats", "t")], writes=[("stats", 24 + c)], desc=f"sqc{c}_{n}")
            else:
                p.op("dve", lambda E, c=c: E.tensor_add(stats[:, 24 + c:25 + c],
                                                        stats[:, 24 + c:25 + c], stats[:, 49:50]),
                     reads=[("stats", "t"), ("stats", 24 + c)],
                     writes=[("stats", 24 + c)], desc=f"sqc{c}_{n}")
        p.op("dve", lambda E, c=c: E.reduce_sum(stats[:, c:c + 1], chunkscr[:], axis=AX.X),
             reads=[("chunkscr", n2) for n2 in range(8)], writes=[("stats", c)],
             desc=f"sums{c}")
        if c >= 8:
            p.dma("pool", xa_d[(c - 8) * 128:(c - 7) * 128, :], chunkscr[:],
                  reads=[("chunkscr", n2) for n2 in range(8)], writes=[("xa_d", c)],
                  desc=f"stxa{c}")

    xaq_d = ctx.dram("xaq_d", [TB, 1024], BF16)
    xaq_t = xaq_d.rearrange("(i p) d -> i p d", p=128)
    for t in range(NTB):
        for dc in range(2):
            bank, bkey = ps[dc], f"psb{dc}"
            mm_chain(ctx, bank[:], bkey,
                     [(lnT_chunk(k, t * 128, 128), lnT_keys(k, t * 128, 128),
                       big2[0:128, k * 3072 + dc * 512:k * 3072 + (dc + 1) * 512],
                       [("WmT", k)])
                      for k in range(KD)],
                     desc=f"xq{t}_{dc}")
            dst = chunkscr[:, dc * 512:(dc + 1) * 512]
            p.op("dve", lambda E, dst=dst, bank=bank: E.tensor_copy(dst, bank[:]),
                 reads=[bkey], writes=[("chunkscr", dc)], desc=f"exq{t}_{dc}")
        p.dma("pool", xaq_t[t], chunkscr[:, 0:1024],
              reads=[("chunkscr", 0), ("chunkscr", 1)], writes=[("xaq", t)],
              desc=f"stxq{t}")

    if phases <= 2:
        return
    sar_i = ctx.dram("sar_i", [128, 48], F32)
    sar_o = ctx.dram("sar_o", [128, 48], F32)
    p.dma("sp", sar_i, stats[:, 0:48],
          reads=[("stats", c) for c in range(48)], writes=["sar_i"], desc="st_stats")
    p.op("pool", lambda E: E.collective_compute(
        "AllReduce", ALU.add, [list(range(NCORES))], [sar_i.opt()], [sar_o.opt()]),
        reads=["sar_i"], writes=["sar_o"], desc="ar_stats")
    p.dma("sp", stats[:, 0:48], sar_o, reads=["sar_o"], writes=[("stats", "all")],
          desc="ld_stats")
    NTOT = 2.0 * NTOK
    p.op("dve", lambda E: E.tensor_scalar(sco[:, 0:24], stats[:, 0:24], 1.0 / NTOT,
                                          None, ALU.mult),
         reads=[("stats", "all")], writes=[("sco", "m")], desc="bn_mean")
    p.op("dve", lambda E: E.tensor_mul(sco[:, 24:48], sco[:, 0:24], sco[:, 0:24]),
         reads=[("sco", "m")], writes=[("sco", "m2")], desc="bn_m2")
    p.op("dve", lambda E: E.tensor_scalar(stats[:, 24:48], stats[:, 24:48], 1.0 / NTOT,
                                          EPS, ALU.mult, ALU.add),
         reads=[("stats", "all")], writes=[("stats", "q")], desc="bn_q")
    p.op("dve", lambda E: E.tensor_sub(stats[:, 24:48], stats[:, 24:48], sco[:, 24:48]),
         reads=[("stats", "q"), ("sco", "m2")], writes=[("stats", "v")], desc="bn_var")
    p.op("act", lambda E: E.activation(out=stats[:, 24:48], in_=stats[:, 24:48],
                                       func=AF.Sqrt),
         reads=[("stats", "v")], writes=[("stats", "sd")], desc="bn_sd")
    p.op("dve", lambda E: E.reciprocal(stats[:, 24:48], stats[:, 24:48]),
         reads=[("stats", "sd")], writes=[("stats", "rstd")], desc="bn_rstd")
    p.op("dve", lambda E: E.tensor_mul(sco[:, 24:48], gb[:, 0:24], stats[:, 24:48]),
         reads=["gb", ("stats", "rstd")], writes=[("sco", "s")], desc="bn_scale")
    p.op("dve", lambda E: E.tensor_mul(sco[:, 0:24], sco[:, 0:24], sco[:, 24:48]),
         reads=[("sco", "m"), ("sco", "s")], writes=[("sco", "ms")], desc="bn_ms")
    p.op("dve", lambda E: E.tensor_sub(sco[:, 0:24], gb[:, 24:48], sco[:, 0:24]),
         reads=["gb", ("sco", "ms")], writes=[("sco", "o")], desc="bn_off")
    p.op("dve", lambda E: E.tensor_scalar(sco[:, 48:56], sco[:, 24:32], 1.0 / 32.0,
                                          None, ALU.mult),
         reads=[("sco", "s")], writes=[("sco", "sq")], desc="bn_sq")
    p.op("dve", lambda E: E.tensor_scalar(sco[:, 56:64], sco[:, 0:8], 1.0 / 32.0,
                                          None, ALU.mult),
         reads=[("sco", "o")], writes=[("sco", "oq")], desc="bn_oq")

    if phases <= 3:
        return
    p.barrier()
    xakbn_d = ctx.dram("xakbn_d", [8 * 128, TB], BF16)
    for c in range(8, 16):
        p.dma("pool", chunkscr[:], xa_d[(c - 8) * 128:(c - 7) * 128, :],
              reads=[("xa_d", c)], writes=["chunkscr"], desc=f"ldk{c}")
        p.op("dve", lambda E, c=c: E.tensor_scalar(
            chunkscr[:], chunkscr[:], sco[:, 24 + c:25 + c], sco[:, c:c + 1],
            ALU.mult, ALU.add),
            reads=["chunkscr"], scalar_reads=[("sco", "s"), ("sco", "o")],
            writes=["chunkscr"], desc=f"bnk{c}")
        p.dma("pool", xakbn_d[(c - 8) * 128:(c - 7) * 128, :], chunkscr[:],
              reads=["chunkscr"], writes=[("xakbn", c)], desc=f"stk{c}")
    for c in range(16, 24):
        j = c - 16
        p.dma("pool", chunkscr[:], xa_d[(c - 8) * 128:(c - 7) * 128, :],
              reads=[("xa_d", c)], writes=["chunkscr"], desc=f"ldv{c}")
        p.op("dve", lambda E, c=c: E.tensor_scalar(
            chunkscr[:], chunkscr[:], sco[:, 24 + c:25 + c], sco[:, c:c + 1],
            ALU.mult, ALU.add),
            reads=["chunkscr"], scalar_reads=[("sco", "s"), ("sco", "o")],
            writes=["chunkscr"], desc=f"bnv{c}")
        p.op("dve", lambda E: E.tensor_copy(lnt[:, 0:1024], chunkscr[:, 0:1024]),
             reads=["chunkscr"], writes=["lnt"], desc=f"cv{c}a")
        for piece in range(4):
            if piece > 0:
                p.op("dve", lambda E, piece=piece: E.tensor_copy(
                    lnt[:, 0:1024], chunkscr[:, piece * 1024:(piece + 1) * 1024]),
                    reads=["chunkscr"], writes=["lnt"], desc=f"cv{c}_{piece}")
            for tt in range(8):
                t = piece * 8 + tt
                transpose_128(ctx, ps_tr, "ps_tr", tt, lnt[:, tt * 128:(tt + 1) * 128],
                              ident, reads=["lnt"])
            for tt in range(8):
                t = piece * 8 + tt
                dst = big1[:, t * 1025 + j * 128:t * 1025 + (j + 1) * 128]
                p.op("dve", lambda E, dst=dst, tt=tt: E.tensor_copy(
                    dst, ps_tr[:, tt * 128:(tt + 1) * 128]),
                    reads=["ps_tr"], writes=[("V", t)], desc=f"ev{c}_{t}")
    for t in range(NTB):
        p.op("pool", lambda E, t=t: E.memset(big1[:, t * 1025 + 1024:t * 1025 + 1025], 1.0),
             writes=[("Vo", t)], desc=f"vones{t}")
    if phases <= 4:
        return
    for t in range(NT):
        p.op("pool", lambda E, t=t: E.indirect_dma_start(
            out=qtb[:], out_offset=None, in_=xaq_d,
            in_offset=bass.IndirectOffsetOnAxis(ap=idxs[:, t:t + 1], axis=0)),
            reads=["idxs", ("xaq", "all")], writes=["qtb"], dma=True, desc=f"gq{t}")
        for k in range(KD):
            transpose_128(ctx, ps_tr, "ps_tr", k, qtb[:, k * 128:(k + 1) * 128],
                          ident, reads=["qtb"])
        for k in range(KD):
            dst = big3[:, k * 2048 + t * 128:k * 2048 + (t + 1) * 128]
            p.op("dve", lambda E, dst=dst, k=k: E.tensor_scalar(
                dst, ps_tr[:, k * 128:(k + 1) * 128], sco[:, 48 + k:49 + k],
                sco[:, 56 + k:57 + k], ALU.mult, ALU.add),
                reads=["ps_tr"], scalar_reads=[("sco", "sq"), ("sco", "oq")],
                writes=[("Q", t, k)], desc=f"eq{t}_{k}")

    if phases <= 5:
        return
    p.barrier()
    Q_keys = [("Q", t, k) for t in range(NT) for k in range(KD)]
    for gq in range(8):
        ptk_all = []
        for kc in range(NTB):
            par = kc % 2
            ksl = kst[:, par * 1024:(par + 1) * 1024]
            for j in range(8):
                srcj = xakbn_d[j * 128:(j + 1) * 128, kc * 128:(kc + 1) * 128]
                p.dma("sp", ksl[:, j * 128:(j + 1) * 128], srcj,
                      reads=[("xakbn", 8 + j)], writes=[("kst", par)],
                      desc=f"ldK{gq}_{kc}_{j}")
            bank, bkey = ps[4 + par], f"psb{4 + par}"
            mm_chain(ctx, bank[:, 0:256], bkey,
                     [(ksl[:, k * 128:(k + 1) * 128], [("kst", par)],
                       big3[:, k * 2048 + gq * 256:k * 2048 + (gq + 1) * 256],
                       Q_keys) for k in range(KD)],
                     desc=f"sc{gq}_{kc}")
            p.op("act", lambda E, bank=bank: E.activation(
                out=escr[:], in_=bank[:, 0:256], func=AF.Exp),
                reads=[bkey], writes=["escr"], desc=f"ex{gq}_{kc}")
            thr = iotac[:, 2:3]
            p.op("dve", lambda E, thr=thr, kc=kc, gq=gq: E.tensor_scalar(
                thr, iotac[:, 1:2], float(kc * 128 - gq * 256), None, ALU.add),
                reads=[("iotac", 1)], writes=[("iotac", 2)], desc=f"th{gq}_{kc}")
            ptb = big2[:, kc * 256:(kc + 1) * 256]
            p.op("dve", lambda E, ptb=ptb, thr=thr: E.scalar_tensor_tensor(
                ptb, qpf[:], thr, escr[:], ALU.is_ge, ALU.mult),
                reads=["qpf", "escr"], scalar_reads=[("iotac", 2)],
                writes=[("PT", kc)], desc=f"pt{gq}_{kc}")
            ptk_all.append(("PT", kc))
        for qs in range(2):
            for vc in range(2):
                bank, bkey = ps[qs * 2 + vc], f"psb{qs * 2 + vc}"
                mm_chain(ctx, bank[:], bkey,
                         [(big2[:, kc * 256 + qs * 128:kc * 256 + qs * 128 + 128],
                           [("PT", kc)],
                           big1[:, kc * 1025 + vc * 512:kc * 1025 + (vc + 1) * 512],
                           [("V", kc)]) for kc in range(NTB)],
                         desc=f"pv{gq}_{qs}_{vc}")
            bank, bkey = ps[4 + qs], f"psb{4 + qs}"
            mm_chain(ctx, bank[:, 0:1], bkey,
                     [(big2[:, kc * 256 + qs * 128:kc * 256 + qs * 128 + 128],
                       [("PT", kc)],
                       onesc[:, 0:1], ["onesc"]) for kc in range(NTB)],
                     desc=f"sum{gq}_{qs}")
            p.op("dve", lambda E, qs=qs, bank=bank: E.reciprocal(
                rcol[:, qs:qs + 1], bank[:, 0:1]),
                reads=[bkey], writes=[("rcol", qs)], desc=f"rc{gq}_{qs}")
            for vc in range(2):
                bank2, bkey2 = ps[qs * 2 + vc], f"psb{qs * 2 + vc}"
                t = gq * 2 + qs
                dst = big4[:, t * 1024 + vc * 512:t * 1024 + (vc + 1) * 512]
                p.op("dve", lambda E, dst=dst, bank2=bank2, qs=qs: E.tensor_scalar(
                    dst, bank2[:], rcol[:, qs:qs + 1], None, ALU.mult),
                    reads=[bkey2], scalar_reads=[("rcol", qs)],
                    writes=[("attn", t)], desc=f"ea{gq}_{qs}_{vc}")

    if phases <= 6:
        return
    p.barrier()
    for k in range(KD):
        dst = big1[:, 16384 + k * 1024:16384 + (k + 1) * 1024]
        p.dma("sp", dst, wpscr[:, k * 128:(k + 1) * 128], reads=["wpscr"],
              writes=[("WpT", k)], desc=f"ldWpT{k}", transpose=True)
        dst2 = big1[:, 24576 + k * 1024:24576 + (k + 1) * 1024]
        p.dma("sp", dst2, wmp_d[:, k * 128:(k + 1) * 128], reads=["wmp_d"],
              writes=[("WmpT", k)], desc=f"ldWmpT{k}", transpose=True)
    for t in range(NT):
        for k in range(KD):
            transpose_128(ctx, ps_tr, "ps_tr", k,
                          big4[:, t * 1024 + k * 128:t * 1024 + (k + 1) * 128],
                          ident, reads=[("attn", t)])
        for k in range(KD):
            dst = big2[:, 8192 + k * 2048 + t * 128:8192 + k * 2048 + (t + 1) * 128]
            p.op("dve", lambda E, dst=dst, k=k: E.tensor_copy(
                dst, ps_tr[:, k * 128:(k + 1) * 128]),
                reads=["ps_tr"], writes=[("attnT", t, k)], desc=f"eaT{t}_{k}")
    attnT_keys = [("attnT", t, k) for t in range(NT) for k in range(KD)]
    for oc in range(KD):
        for g5 in range(4):
            bank, bkey = ps[g5 % 2], f"psb{g5 % 2}"
            mm_chain(ctx, bank[:], bkey,
                     [(big1[:, 16384 + k * 1024 + oc * 128:16384 + k * 1024 + (oc + 1) * 128],
                       [("WpT", k)],
                       big2[:, 8192 + k * 2048 + g5 * 512:8192 + k * 2048 + (g5 + 1) * 512],
                       attnT_keys) for k in range(KD)],
                     desc=f"y1_{oc}_{g5}")
            dst = big3[:, oc * 2048 + g5 * 512:oc * 2048 + (g5 + 1) * 512]
            p.op("dve", lambda E, dst=dst, bank=bank: E.tensor_copy(dst, bank[:]),
                 reads=[bkey], writes=[("y1", oc, g5)], desc=f"ey1_{oc}_{g5}")
    y1_keys = [("y1", oc, g5) for oc in range(KD) for g5 in range(4)]
    for t in range(NT):
        p.dma("sp", xtile[:], xo_t[t], writes=[("x", 0)], desc=f"ldxo{t}")
        for dc in range(2):
            bank, bkey = ps[4 + dc], f"psb{4 + dc}"
            mm_chain(ctx, bank[:], bkey,
                     [(big3[:, k * 2048 + t * 128:k * 2048 + (t + 1) * 128], y1_keys,
                       big1[:, 24576 + k * 1024 + dc * 512:24576 + k * 1024 + (dc + 1) * 512],
                       [("WmpT", k)]) for k in range(KD)],
                     desc=f"y2_{t}_{dc}")
            dst = xtile[:, dc * 512:(dc + 1) * 512]
            p.op("dve", lambda E, dst=dst, bank=bank: E.tensor_add(dst, dst, bank[:]),
                 reads=[bkey, ("x", 0)], writes=[("x", 0)], desc=f"res{t}_{dc}")
        p.dma("sp", x_out_t[t], xtile[:], reads=[("x", 0)], writes=[("xo", t)],
              desc=f"stx{t}")


def build_L1_ffn(ctx, core, x_in, x_out, W):
    nc, p = ctx.nc, ctx.p
    ident = make_identity(ctx)

    lnTt = ctx.sb("lnTt", [128, 1024], BF16)
    lnt = ctx.sb("lnt", [128, D], BF16)
    xtile = ctx.sb("xtile", [128, D], F32)
    lnscr = ctx.sb("lnscr", [128, D], F32)
    stat = ctx.sb("stat", [128, 8], F32)
    svec = ctx.sb("svec", [128, 64], F32)
    scoresB = ctx.sb("scoresB", [128, 4096], F32)
    cmpt = ctx.sb("cmpt", [128, 4096], F32)
    srow = cmpt
    wrc = ctx.sb("wrc", [128, 8], BF16)
    onesr = ctx.sb("onesr", [128, 128], BF16)
    onesf1 = ctx.sb("onesf1", [128, 128], F32)
    b1r = ctx.sb("b1r", [128, 4096], BF16)
    b2r = ctx.sb("b2r", [128, 1024], BF16)
    w1 = ctx.sb("w1", [128, 8 * 4096], BF16)
    w2 = ctx.sb("w2", [128, 32 * 1024], BF16)
    htile = ctx.sb("htile", [128, 4096], BF16)
    hT = ctx.sb("hT", [128, 4096], BF16)
    gsb = ctx.sb("gsb", [128, 512 * 2], F32)
    gu = ctx.sb("gu", [128, 512 * 2], F32)
    ps0 = ctx.ps("ps0", [128, 512], F32)
    ps1 = ctx.ps("ps1", [128, 512], F32)
    ps2 = ctx.ps("ps2", [128, 512], F32)
    ps3 = ctx.ps("ps3", [128, 512], F32)
    ps_tr = ctx.ps("ps_tr", [128, 1024], BF16)

    x_in_t = x_in.rearrange("(i p) d -> i p d", p=128)
    x_out_t = x_out.rearrange("(i p) d -> i p d", p=128)

    p.op("pool", lambda E: E.memset(onesr[:], 1.0), writes=["onesr"], desc="ones")
    p.op("pool", lambda E: E.memset(onesf1[:], 1.0), writes=["onesf1"], desc="onesf1")
    p.dma("pool", wrc[:], W["w_router"].rearrange("(k p) -> p k", p=128),
          writes=["wrc"], desc="wrc", allow_slow_non_contiguous=True)
    p.dma("pool", b1r[0:1, :], W["b1"][None, :], writes=["b1r"], desc="b1r")
    p.dma("pool", b2r[0:1, :], W["b2"][None, :], writes=["b2r"], desc="b2r")
    p.op("dve", lambda E: E.tensor_scalar(b2r[0:1, :], b2r[0:1, :], 2.0, None, ALU.mult),
         reads=["b2r"], writes=["b2r"], desc="b2x2")
    src = W["W1"].rearrange("(k p) n -> p k n", p=128)
    p.dma("pool", w1[:].rearrange("p (k n) -> p k n", n=4096), src,
          writes=["w1"], desc="ldw1")
    src = W["W2"].rearrange("(k p) n -> p k n", p=128)
    p.dma("pool", w2[:].rearrange("p (k n) -> p k n", n=1024), src,
          writes=["w2"], desc="ldw2")

    s_in = ctx.dram("s_in", [TLOC, 1], F32)
    s_out = ctx.dram("s_out", [T, 1], F32)

    for i in range(NT):
        p.dma("sp", xtile[:], x_in_t[i], writes=[("x", 0)], desc=f"ldx{i}")
        layernorm_tile(ctx, lnt[:], xtile[:], lnscr, stat, i, src_key=("x", 0),
                       dst_key=("ln", 0))
        p.op("dve", lambda E, i=i: E.tensor_scalar(
            svec[:, 32 + i:33 + i], stat[:, 6:7], 0.5, 1.0, ALU.mult, ALU.add),
            reads=[("lnstat", i)], writes=[("svec", 32 + i)], desc=f"cs{i}")
        p.op("dve", lambda E, i=i: E.tensor_mul(
            svec[:, 48 + i:49 + i], stat[:, 0:1], stat[:, 6:7]),
            reads=[("lnstat", i)], writes=[("svec", 48 + i)], desc=f"co0{i}")
        p.op("dve", lambda E, i=i: E.tensor_scalar(
            svec[:, 48 + i:49 + i], svec[:, 48 + i:49 + i], 0.5, None, ALU.mult),
            reads=[("svec", 48 + i)], writes=[("svec", 48 + i)], desc=f"co{i}")
        for k in range(KD):
            transpose_128(ctx, ps_tr, "ps_tr", k, lnt[:, k * 128:(k + 1) * 128],
                          ident, reads=[("ln", 0)])
        p.op("dve", lambda E: E.tensor_copy(lnTt[:], ps_tr[:]),
             reads=["ps_tr"], writes=["lnTt"], desc=f"elnT{i}")
        mm_chain(ctx, ps0[:, 0:1], "ps0",
                 [(lnTt[:, k * 128:(k + 1) * 128], ["lnTt"],
                   wrc[:, k:k + 1], ["wrc"]) for k in range(KD)],
                 desc=f"sc{i}")
        p.op("dve", lambda E, i=i: E.tensor_copy(svec[:, i:i + 1], ps0[:, 0:1]),
             reads=["ps0"], writes=[("svec", i)], desc=f"esc{i}")
    p.dma("sp", s_in.rearrange("(t p) one -> p (t one)", p=128), svec[:, 0:16],
          reads=[("svec", i) for i in range(16)], writes=["s_in"], desc="st_s",
          allow_slow_non_contiguous=True)
    groups = [[2 * g, 2 * g + 1] for g in range(4)]
    p.op("pool", lambda E: E.collective_compute(
        "AllGather", ALU.bypass, groups, [s_in.opt()], [s_out.opt()]),
        reads=["s_in"], writes=["s_out"], desc="ag_s")
    p.dma("sp", srow[0:1, :], s_out.rearrange("a one -> one a"),
          reads=["s_out"], writes=["srow"], desc="ld_srow")
    for n in range(8):
        p.op("pe", lambda E, n=n: E.matmul(ps0[:], onesf1[0:1, :],
                                           srow[0:1, n * 512:(n + 1) * 512],
                                           start=True, stop=True),
             reads=["srow", "onesf1"], writes=["ps0"], desc=f"bcs{n}")
        p.op("dve", lambda E, n=n: E.tensor_copy(scoresB[:, n * 512:(n + 1) * 512], ps0[:]),
             reads=["ps0"], writes=["scoresB"], desc=f"ebcs{n}")
    for i in range(NT):
        p.op("dve", lambda E, i=i: E.tensor_scalar(
            cmpt[:], scoresB[:], svec[:, i:i + 1], None, ALU.is_ge),
            reads=["scoresB"], scalar_reads=[("svec", i)], writes=["cmpt"],
            desc=f"cmp{i}")
        p.op("dve", lambda E: E.reduce_sum(stat[:, 0:1], cmpt[:], axis=AX.X),
             reads=["cmpt"], writes=[("stat", "rank")], desc=f"rk{i}")
        p.op("dve", lambda E: E.tensor_scalar(stat[:, 1:2], stat[:, 0:1], 2048.5,
                                              None, ALU.is_le),
             reads=[("stat", "rank")], writes=[("stat", "m")], desc=f"msk{i}")
        p.op("act", lambda E, i=i: E.activation(out=stat[:, 2:3], in_=svec[:, i:i + 1],
                                                func=AF.Sigmoid),
             scalar_reads=[("svec", i)], writes=[("stat", "sg")], desc=f"sg{i}")
        p.op("dve", lambda E: E.tensor_mul(stat[:, 1:2], stat[:, 1:2], stat[:, 2:3]),
             reads=[("stat", "m"), ("stat", "sg")], writes=[("stat", "m")],
             desc=f"wm{i}")
        p.op("dve", lambda E, i=i: E.tensor_scalar(
            svec[:, 16 + i:17 + i], stat[:, 1:2], 0.25, None, ALU.mult),
            reads=[("stat", "m")], writes=[("svec", 16 + i)], desc=f"wh{i}")

    GC = 0.7978845608028654
    for i in range(NT):
        p.dma("sp", xtile[:], x_in_t[i], writes=[("x", 0)], desc=f"ldx2{i}")
        layernorm_tile(ctx, lnt[:], xtile[:], lnscr, stat, 100 + i, src_key=("x", 0),
                       dst_key=("ln", 0))
        for k in range(KD):
            transpose_128(ctx, ps_tr, "ps_tr", k, lnt[:, k * 128:(k + 1) * 128],
                          ident, reads=[("ln", 0)])
        p.op("dve", lambda E: E.tensor_copy(lnTt[:], ps_tr[:]),
             reads=["ps_tr"], writes=["lnTt"], desc=f"elnT2{i}")
        for n in range(8):
            bank, bkey = (ps0, "ps0") if n % 2 == 0 else (ps1, "ps1")
            pairs = [(lnTt[:, k * 128:(k + 1) * 128], ["lnTt"],
                      w1[:, k * 4096 + n * 512:k * 4096 + (n + 1) * 512], ["w1"])
                     for k in range(KD)]
            pairs.append((onesr[0:1, :], ["onesr"],
                          b1r[0:1, n * 512:(n + 1) * 512], ["b1r"]))
            mm_chain(ctx, bank[:], bkey, pairs, desc=f"w1_{i}_{n}")
            db = n % 2
            g0 = gsb[:, db * 512:(db + 1) * 512]
            u = gu[:, 0:512]
            a = gu[:, 512:1024]
            p.op("dve", lambda E, g0=g0, bank=bank: E.tensor_copy(g0, bank[:]),
                 reads=[bkey], writes=[("gsb", db)], desc=f"ge0_{i}_{n}")
            p.op("dve", lambda E, g0=g0, u=u: E.tensor_mul(u, g0, g0),
                 reads=[("gsb", db)], writes=[("gu", 0)], desc=f"ge1_{i}_{n}")
            p.op("dve", lambda E, u=u, a=a: E.tensor_scalar(
                a, u, 0.044715, 1.0, ALU.mult, ALU.add),
                reads=[("gu", 0)], writes=[("gu", 1)], desc=f"ge2_{i}_{n}")
            p.op("dve", lambda E, g0=g0, a=a, u=u: E.tensor_mul(u, a, g0),
                 reads=[("gu", 1), ("gsb", db)], writes=[("gu", 0)], desc=f"ge3_{i}_{n}")
            p.op("act", lambda E, u=u, a=a: E.activation(out=a, in_=u, func=AF.Tanh,
                                                         scale=GC),
                 reads=[("gu", 0)], writes=[("gu", 1)], desc=f"ge4_{i}_{n}")
            hsl = htile[:, n * 512:(n + 1) * 512]
            p.op("dve", lambda E, hsl=hsl, a=a, g0=g0: E.scalar_tensor_tensor(
                hsl, a, 1.0, g0, ALU.add, ALU.mult),
                reads=[("gu", 1), ("gsb", db)], writes=[("htile", n)], desc=f"ge6_{i}_{n}")
        for piece in range(4):
            for kk in range(8):
                transpose_128(ctx, ps_tr, "ps_tr", kk,
                              htile[:, piece * 1024 + kk * 128:piece * 1024 + (kk + 1) * 128],
                              ident, reads=[("htile", piece * 2), ("htile", piece * 2 + 1)])
            p.op("dve", lambda E, piece=piece: E.tensor_copy(
                hT[:, piece * 1024:(piece + 1) * 1024], ps_tr[:]),
                reads=["ps_tr"], writes=[("hT", piece)], desc=f"ehT{i}_{piece}")
        for hf in range(2):
            bank, bkey = (ps2, "ps2") if hf == 0 else (ps3, "ps3")
            pairs = [(hT[:, k * 128:(k + 1) * 128], [("hT", k // 8)],
                      w2[:, k * 1024 + hf * 512:k * 1024 + hf * 512 + 512], ["w2"])
                     for k in range(32)]
            pairs.append((onesr[0:1, :], ["onesr"],
                          b2r[0:1, hf * 512:(hf + 1) * 512], ["b2r"]))
            mm_chain(ctx, bank[:], bkey, pairs, desc=f"w2_{i}_{hf}")
        p.op("dve", lambda E, i=i: E.tensor_scalar(
            xtile[:], xtile[:], svec[:, 32 + i:33 + i], svec[:, 48 + i:49 + i],
            ALU.mult, ALU.subtract),
            reads=[("x", 0)], scalar_reads=[("svec", 32 + i), ("svec", 48 + i)],
            writes=[("x", 0)], desc=f"cb0{i}")
        for hf in range(2):
            bank, bkey = (ps2, "ps2") if hf == 0 else (ps3, "ps3")
            dst = xtile[:, hf * 512:(hf + 1) * 512]
            p.op("dve", lambda E, dst=dst, bank=bank, i=i: E.scalar_tensor_tensor(
                dst, bank[:], svec[:, 16 + i:17 + i], dst, ALU.mult, ALU.add),
                reads=[bkey, ("x", 0)], scalar_reads=[("svec", 16 + i)],
                writes=[("x", 0)], desc=f"cb{i}_{hf}")
        p.dma("sp", x_out_t[i], xtile[:], reads=[("x", 0)], writes=[("xo", i)],
              desc=f"stx{i}")


def build_L0_ffn(ctx, core, xb_in, xown_in, x_out, W):
    nc, p = ctx.nc, ctx.p
    ident = make_identity(ctx)
    TB = 4096
    NTB = 32

    big1 = ctx.sb("big1", [128, 32768], BF16)
    v2 = ctx.sb("v2", [128, NTB * 520], BF16)
    qfm = ctx.sb("qfm", [128, 4 * 2048], BF16)
    att = ctx.sb("att", [128, NT * 520], BF16)
    lnTo = ctx.sb("lnTo", [128, 1024], BF16)
    lnt = ctx.sb("lnt", [128, D], BF16)
    xtile = ctx.sb("xtile", [128, D], F32)
    lnscr = ctx.sb("lnscr", [128, D], F32)
    stat = ctx.sb("stat", [128, 8], F32)
    glv = ctx.sb("glv", [128, 96], F32)
    wvs = ctx.sb("wvs", [128, 40], F32)
    mko = ctx.sb("mko", [128, NT * 40], F32)
    xvs = ctx.sb("xvs", [128, 320], F32)
    rc8 = ctx.sb("rc8", [128, 8], F32)
    usb = ctx.sb("usb", [128, 384], BF16)
    uT = ctx.sb("uT", [128, 384], BF16)
    w_q = ctx.sb("w_q", [128, 8 * 512], BF16)
    w_k = ctx.sb("w_k", [128, 8 * 512], BF16)
    w_s = ctx.sb("w_s", [128, 8 * 40], BF16)
    w_d = ctx.sb("w_d", [128, 8 * 40], BF16)
    w_v = ctx.sb("w_v", [128, 8 * 320], BF16)
    w_o = ctx.sb("w_o", [128, 3 * 1024], BF16)
    ps0 = ctx.ps("ps0", [128, 512], F32)
    ps1 = ctx.ps("ps1", [128, 512], F32)
    ps2 = ctx.ps("ps2", [128, 512], F32)
    ps3 = ctx.ps("ps3", [128, 512], F32)
    ps4 = ctx.ps("ps4", [128, 512], F32)
    ps5 = ctx.ps("ps5", [128, 512], F32)
    ps_tr = ctx.ps("ps_tr", [128, 1024], BF16)

    xb_t = xb_in.rearrange("(i p) d -> i p d", p=128)
    xo_t = xown_in.rearrange("(i p) d -> i p d", p=128)
    x_out_t = x_out.rearrange("(i p) d -> i p d", p=128)

    for wsb, nm, hb, cols in ((w_q, "w_q", W["Wq"], 512), (w_k, "w_k", W["Wk"], 512),
                              (w_s, "w_s", W["Ws"], 40), (w_d, "w_d", W["Wd"], 40)):
        src = hb.rearrange("(k p) n -> p k n", p=128)
        p.dma("pool", wsb[:].rearrange("p (k n) -> p k n", n=cols), src,
              writes=[nm], desc=f"ld{nm}")
    for e in range(5):
        src = W["Wv"][e].rearrange("(k p) n -> p k n", p=128)
        dst = w_v[:].rearrange("p (k en) -> p k en", en=320)[:, :, e * 64:(e + 1) * 64]
        p.dma("pool", dst, src, writes=["w_v"], desc=f"ldwv{e}")
    p.op("pool", lambda E: E.memset(glv[:, 93:96], -1e30), writes=[("glv", "pad")],
         desc="glvpad")
    wo_flat = W["Wo"].rearrange("e k n -> (e k) n")
    for ch in range(3):
        r0, r1 = ch * 128, min(320, (ch + 1) * 128)
        p.dma("pool", w_o[0:r1 - r0, ch * 1024:(ch + 1) * 1024], wo_flat[r0:r1, :],
              writes=["w_o"], desc=f"ldwo{ch}")

    def glv_gates(tag, dst_sb, dst_key, chunk_fn, ckeys_fn, wsb, wkey, sigmoid):
        mm_chain(ctx, ps5[:, 0:40], "ps5",
                 [(chunk_fn(k), ckeys_fn(k),
                   wsb[:, k * 40:(k + 1) * 40], [wkey]) for k in range(KD)],
                 desc=f"gl{tag}")
        p.op("dve", lambda E: E.tensor_copy(glv[:, 0:40], ps5[:, 0:40]),
             reads=["ps5"], writes=[("glv", "v")], desc=f"eg{tag}")
        if sigmoid:
            p.op("act", lambda E: E.activation(out=glv[:, 48:88], in_=glv[:, 0:40],
                                               func=AF.Sigmoid),
                 reads=[("glv", "v")], writes=[("glv", "sg")], desc=f"sg{tag}")
        for h in range(8):
            sl = glv[:, h * 5:h * 5 + 5]
            p.op("dve", lambda E, sl=sl: E.tensor_copy(glv[:, 88:93], sl),
                 reads=[("glv", "v")], writes=[("glv", "p8")], desc=f"cp{tag}_{h}")
            p.op("dve", lambda E: E.max(out=glv[:, 40:48], in_=glv[:, 88:96]),
                 reads=[("glv", "p8"), ("glv", "pad")], writes=[("glv", "m")],
                 desc=f"mx{tag}_{h}")
            dsl = dst_sb[:, h * 5:h * 5 + 5]
            p.op("dve", lambda E, sl=sl, dsl=dsl: E.tensor_scalar(
                dsl, sl, glv[:, 41:42], None, ALU.is_ge),
                reads=[("glv", "v")], scalar_reads=[("glv", "m")],
                writes=[dst_key], desc=f"mk{tag}_{h}")
        if sigmoid:
            p.op("dve", lambda E: E.tensor_mul(dst_sb[:, 0:40], dst_sb[:, 0:40],
                                               glv[:, 48:88]),
                 reads=[dst_key, ("glv", "sg")], writes=[dst_key], desc=f"wm{tag}")

    for i in range(NTB):
        p.dma("sp", xtile[:], xb_t[i], writes=[("x", 0)], desc=f"ldx{i}")
        layernorm_tile(ctx, lnt[:], xtile[:], lnscr, stat, i, src_key=("x", 0),
                       dst_key=("ln", 0))
        for k in range(KD):
            transpose_128(ctx, ps_tr, "ps_tr", k, lnt[:, k * 128:(k + 1) * 128],
                          ident, reads=[("ln", 0)])
        for k in range(KD):
            dst = big1[:, k * TB + i * 128:k * TB + (i + 1) * 128]
            p.op("dve", lambda E, dst=dst, k=k: E.tensor_copy(
                dst, ps_tr[:, k * 128:(k + 1) * 128]),
                reads=["ps_tr"], writes=[("lnT", i, k)], desc=f"elnT{i}_{k}")

    def lnT_chunk(k, t0, n):
        return big1[:, k * TB + t0:k * TB + t0 + n]

    def lnT_keys(k, t0, n):
        return [("lnT", t, k) for t in range(t0 // 128, (t0 + n) // 128)]

    for t in range(NTB):
        for e in range(5):
            mm_chain(ctx, ps0[:, e * 64:(e + 1) * 64], "ps0",
                     [(lnT_chunk(k, t * 128, 128), lnT_keys(k, t * 128, 128),
                       w_v[:, k * 320 + e * 64:k * 320 + (e + 1) * 64], ["w_v"])
                      for k in range(KD)],
                     desc=f"xv{t}_{e}")
        p.op("dve", lambda E: E.tensor_copy(xvs[:, 0:320], ps0[:, 0:320]),
             reads=["ps0"], writes=["xvs"], desc=f"exv{t}")
        glv_gates(f"v{t}", wvs, "wvs2",
                  lambda k, t=t: lnT_chunk(k, t * 128, 128),
                  lambda k, t=t: lnT_keys(k, t * 128, 128), w_s, "w_s", True)
        for h in range(8):
            vdst = v2[:, t * 520 + h * 65:t * 520 + h * 65 + 64]
            p.op("dve", lambda E, vdst=vdst, h=h: E.tensor_scalar(
                vdst, xvs[:, 0:64], wvs[:, h * 5:h * 5 + 1], None, ALU.mult),
                reads=["xvs"], scalar_reads=["wvs2"], writes=[("v2", t)],
                desc=f"v{t}_{h}_0")
            for e in range(1, 5):
                p.op("dve", lambda E, vdst=vdst, h=h, e=e: E.scalar_tensor_tensor(
                    vdst, xvs[:, e * 64:(e + 1) * 64], wvs[:, h * 5 + e:h * 5 + e + 1],
                    vdst, ALU.mult, ALU.add),
                    reads=["xvs", ("v2", t)], scalar_reads=["wvs2"],
                    writes=[("v2", t)], desc=f"v{t}_{h}_{e}")
        p.op("pool", lambda E, t=t: E.memset(v2[:, t * 520 + 64:t * 520 + 65], 1.0),
             writes=[("v2", t)], desc=f"vo{t}a")
        for h in range(1, 8):
            p.op("pool", lambda E, t=t, h=h: E.tensor_copy(
                v2[:, t * 520 + h * 65 + 64:t * 520 + h * 65 + 65],
                v2[:, t * 520 + 64:t * 520 + 65]),
                reads=[("v2", t)], writes=[("v2", t)], desc=f"vo{t}_{h}")

    kbuf = ctx.sb("kbuf", [128, 4 * TB], BF16)
    for pp in range(4):
        for g in range(TB // 512):
            for hh in range(2):
                bank, bkey = (ps0, "ps0") if hh == 0 else (ps1, "ps1")
                mm_chain(ctx, bank[hh * 64:hh * 64 + 64, :], bkey,
                         [(w_k[:, k * 512 + pp * 128 + hh * 64:k * 512 + pp * 128 + hh * 64 + 64],
                           ["w_k"],
                           lnT_chunk(k, g * 512, 512), lnT_keys(k, g * 512, 512))
                          for k in range(KD)],
                         desc=f"k{pp}_{g}_{hh}")
                dst = kbuf[hh * 64:hh * 64 + 64, pp * TB + g * 512:pp * TB + (g + 1) * 512]
                p.op("dve", lambda E, dst=dst, bank=bank, hh=hh: E.tensor_copy(
                    dst, bank[hh * 64:hh * 64 + 64, :]),
                    reads=[bkey], writes=[("kb", pp, g)], desc=f"ek{pp}_{g}_{hh}")

    for i in range(NT):
        p.dma("sp", xtile[:], xo_t[i], writes=[("x", 0)], desc=f"ldxo{i}")
        layernorm_tile(ctx, lnt[:], xtile[:], lnscr, stat, 50 + i, src_key=("x", 0),
                       dst_key=("ln", 0))
        for k in range(KD):
            transpose_128(ctx, ps_tr, "ps_tr", k, lnt[:, k * 128:(k + 1) * 128],
                          ident, reads=[("ln", 0)])
        p.op("dve", lambda E: E.tensor_copy(lnTo[:], ps_tr[:]),
             reads=["ps_tr"], writes=["lnTo"], desc=f"elnTo{i}")
        for pp in range(4):
            for hh in range(2):
                bank, bkey = (ps0, "ps0") if hh == 0 else (ps1, "ps1")
                mm_chain(ctx, bank[hh * 64:hh * 64 + 64, 0:128], bkey,
                         [(w_q[:, k * 512 + pp * 128 + hh * 64:k * 512 + pp * 128 + hh * 64 + 64],
                           ["w_q"],
                           lnTo[:, k * 128:(k + 1) * 128], ["lnTo"])
                          for k in range(KD)],
                         desc=f"q{i}_{pp}_{hh}")
                dst = qfm[hh * 64:hh * 64 + 64, pp * 2048 + i * 128:pp * 2048 + (i + 1) * 128]
                p.op("dve", lambda E, dst=dst, bank=bank, hh=hh: E.tensor_scalar(
                    dst, bank[hh * 64:hh * 64 + 64, 0:128], 0.125, None, ALU.mult),
                    reads=[bkey], writes=[("qf", pp, i)], desc=f"eq{i}_{pp}_{hh}")
        mdst = mko[:, i * 40:(i + 1) * 40]
        glv_gates(f"o{i}", mdst, ("mko", i),
                  lambda k: lnTo[:, k * 128:(k + 1) * 128],
                  lambda k: ["lnTo"], w_d, "w_d", False)
    p.barrier()
    for gq in range(4):
        for h in range(8):
            pp, hh = h // 2, h % 2
            for kc in range(NTB):
                sbank, sbkey = (ps0, "ps0") if kc % 2 == 0 else (ps1, "ps1")
                lt_ = kbuf[hh * 64:hh * 64 + 64, pp * TB + kc * 128:pp * TB + (kc + 1) * 128]
                rh_ = qfm[hh * 64:hh * 64 + 64, pp * 2048 + gq * 512:pp * 2048 + (gq + 1) * 512]
                p.op("pe", lambda E, b=sbank, lt_=lt_, rh_=rh_: E.matmul(
                    b[:], lt_, rh_, start=True, stop=True),
                    reads=[("kb", pp, kc // 4)] + [("qf", pp, gq * 4 + qq) for qq in range(4)],
                    writes=[sbkey], desc=f"sc{gq}_{h}_{kc}")
                ptb = big1[:, 16384 + kc * 512:16384 + (kc + 1) * 512]
                p.op("act", lambda E, b=sbank, ptb=ptb: E.activation(
                    out=ptb, in_=b[:], func=AF.Exp),
                    reads=[sbkey], writes=[("PT", kc)], desc=f"ex{gq}_{h}_{kc}")
            for qs in range(4):
                t = gq * 4 + qs
                bank, bkey = (ps2, "ps2") if qs % 2 == 0 else (ps3, "ps3")
                mm_chain(ctx, bank[:, 0:65], bkey,
                         [(big1[:, 16384 + kc * 512 + qs * 128:16384 + kc * 512 + qs * 128 + 128],
                           [("PT", kc)],
                           v2[:, kc * 520 + h * 65:kc * 520 + (h + 1) * 65], [("v2", kc)])
                          for kc in range(NTB)],
                         desc=f"pv{gq}_{h}_{qs}")
                p.op("dve", lambda E, bank=bank, h=h: E.reciprocal(
                    rc8[:, h:h + 1], bank[:, 64:65]),
                    reads=[bkey], writes=[("rc8", h)], desc=f"rc{gq}_{qs}_{h}")
                adst = att[:, t * 520 + h * 65:t * 520 + h * 65 + 64]
                p.op("dve", lambda E, adst=adst, bank=bank, h=h: E.tensor_scalar(
                    adst, bank[:, 0:64], rc8[:, h:h + 1], None, ALU.mult),
                    reads=[bkey], scalar_reads=[("rc8", h)],
                    writes=[("att", t)], desc=f"ea{gq}_{qs}_{h}")

    for i in range(NT):
        for e in range(5):
            udst = usb[:, e * 64:(e + 1) * 64]
            p.op("dve", lambda E, udst=udst, i=i, e=e: E.tensor_scalar(
                udst, att[:, i * 520 + 0 * 65:i * 520 + 64], mko[:, i * 40 + e:i * 40 + e + 1],
                None, ALU.mult),
                reads=[("att", i)], scalar_reads=[("mko", i)], writes=[("usb", e)],
                desc=f"u{i}_{e}_0")
            for h in range(1, 8):
                p.op("dve", lambda E, udst=udst, i=i, e=e, h=h: E.scalar_tensor_tensor(
                    udst, att[:, i * 520 + h * 65:i * 520 + h * 65 + 64],
                    mko[:, i * 40 + h * 5 + e:i * 40 + h * 5 + e + 1],
                    udst, ALU.mult, ALU.add),
                    reads=[("att", i), ("usb", e)], scalar_reads=[("mko", i)],
                    writes=[("usb", e)], desc=f"u{i}_{e}_{h}")
        p.op("dve", lambda E: E.memset(usb[:, 320:384], 0.0),
             writes=[("usb", "pad")], desc=f"upad{i}")
        for ch in range(3):
            transpose_128(ctx, ps_tr, "ps_tr", ch, usb[:, ch * 128:(ch + 1) * 128],
                          ident, reads=[("usb", e2) for e2 in range(5)] + [("usb", "pad")])
        p.op("dve", lambda E: E.tensor_copy(uT[:, 0:384], ps_tr[:, 0:384]),
             reads=["ps_tr"], writes=["uT"], desc=f"euT{i}")
        p.dma("sp", xtile[:], xo_t[i], writes=[("x", 0)], desc=f"ldxo2{i}")
        for hf in range(2):
            bank, bkey = (ps4, "ps4") if hf == 0 else (ps5, "ps5")
            mm_chain(ctx, bank[:], bkey,
                     [(uT[:, ch * 128:ch * 128 + (128 if ch < 2 else 64)][0:128 if ch < 2 else 64, :]
                       if False else uT[0:(128 if ch < 2 else 64), ch * 128:(ch + 1) * 128],
                       ["uT"],
                       w_o[0:(128 if ch < 2 else 64), ch * 1024 + hf * 512:ch * 1024 + hf * 512 + 512],
                       ["w_o"]) for ch in range(3)],
                     desc=f"y{i}_{hf}")
            dst = xtile[:, hf * 512:(hf + 1) * 512]
            p.op("dve", lambda E, dst=dst, bank=bank: E.scalar_tensor_tensor(
                dst, bank[:], 0.5, dst, ALU.mult, ALU.add),
                reads=[bkey, ("x", 0)], writes=[("x", 0)], desc=f"res{i}_{hf}")
        p.dma("sp", x_out_t[i], xtile[:], reads=[("x", 0)], writes=[("xo", i)],
              desc=f"stx{i}")


def build_L1_attn(ctx, core, xh_in, xown_in, x_out, W, hmask_in):
    nc, p = ctx.nc, ctx.p
    ident = make_identity(ctx)
    TH = TLOC + 128
    NTH = TH // 128

    big_ln = ctx.sb("big_ln", [128, KD * TH], BF16)
    gateT = ctx.sb("gateT", [128, 16 * 2048], BF16)
    bigB = ctx.sb("bigB", [128, 16 * 2048], BF16)
    xcT = ctx.sb("xcT", [128, 2176], F32)
    sc1 = ctx.sb("sc1", [128, 2048], F32)
    sc2 = ctx.sb("sc2", [128, 2048], F32)
    sc3 = ctx.sb("sc3", [128, 2048], F32)
    sc4 = ctx.sb("sc4", [128, 2048], F32)
    zeros2k = xcT
    lnt = bigB[:, 0:1024]
    xtile = sc2[:, 0:1024]
    lnscr = sc1[:, 0:1024]
    stat = ctx.sb("stat", [128, 8], F32)
    cvw = ctx.sb("cvw", [128, 64], F32)
    cvb = ctx.sb("cvb", [128, 16], F32)
    afb = ctx.sb("afb", [128, 16], F32)
    bgc = ctx.sb("bgc", [128, 32], F32)
    carry = ctx.sb("carry", [128, 16], F32)
    carin = ctx.sb("carin", [128, 16], F32)
    hmask = ctx.sb("hmask", [128, 1], F32)
    wst = ctx.sb("wst", [128, 2 * 1024], BF16)
    wgt = big_ln[:, 0:4096]

    ps0 = ctx.ps("ps0", [128, 512], F32)
    ps1 = ctx.ps("ps1", [128, 512], F32)
    ps2 = ctx.ps("ps2", [128, 512], F32)
    ps3 = ctx.ps("ps3", [128, 512], F32)
    ps_tr = ctx.ps("ps_tr", [128, 1024], BF16)

    xh_t = xh_in.rearrange("(i p) d -> i p d", p=128)
    xo_t = xown_in.rearrange("(i p) d -> i p d", p=128)
    x_out_t = x_out.rearrange("(i p) d -> i p d", p=128)


    p.dma("sp", hmask[:], hmask_in, writes=["hmask"], desc="hmask")
    p.dma("pool", cvw[:].rearrange("p (c k) -> p c k", k=4),
          W["conv_w"].rearrange("(c p) one k -> p c (one k)", p=128),
          writes=["cvw"], desc="cvw")
    p.dma("sp", cvb[:], W["conv_b"].rearrange("(c p) -> p c", p=128),
          writes=["cvb"], desc="cvb", allow_slow_non_contiguous=True)
    p.dma("sp", afb[:], W["forget_base"].rearrange("(c p) -> p c", p=128),
          writes=[("afb", 0)], desc="fb", allow_slow_non_contiguous=True)
    p.dma("sp", bgc[:], W["b_g"].rearrange("(c p) -> p c", p=128),
          writes=["bgc"], desc="bg", allow_slow_non_contiguous=True)
    p.op("act", lambda E: E.activation(out=afb[:], in_=afb[:], func=AF.Exp),
         reads=[("afb", 0)], writes=[("afb", 1)], desc="fb_e")
    p.op("act", lambda E: E.activation(out=afb[:], in_=afb[:], func=AF.Ln, bias=1.0),
         reads=[("afb", 1)], writes=[("afb", 2)], desc="fb_l")
    p.op("dve", lambda E: E.tensor_scalar(afb[:], afb[:], -8.0, None, ALU.mult),
         reads=[("afb", 2)], writes=["afb"], desc="fb_s")

    winscr, _ = cast_weight_scratch(ctx, "winscr", W["W_in"], 4096, 1024)
    wgscr, _ = cast_weight_scratch(ctx, "wgscr", W["W_g"], 4096, 2048)
    woscr, _ = cast_weight_scratch(ctx, "woscr", W["W_out"], 1024, 2048)
    wot = big_ln

    for i in range(NTH):
        p.dma("sp", xtile[:], xh_t[i], writes=[("x", 0)], desc=f"ldx{i}")
        layernorm_tile(ctx, lnt[:], xtile[:], lnscr, stat, i, src_key=("x", 0),
                       dst_key=("ln", 0))
        for k in range(KD):
            transpose_128(ctx, ps_tr, "ps_tr", k, lnt[:, k * 128:(k + 1) * 128],
                          ident, reads=[("ln", 0)])
        for k in range(KD):
            dst = big_ln[:, k * TH + i * 128:k * TH + (i + 1) * 128]
            p.op("dve", lambda E, dst=dst, k=k: E.tensor_copy(
                dst, ps_tr[:, k * 128:(k + 1) * 128]),
                reads=["ps_tr"], writes=[("lnT", i, k)], desc=f"elnT{i}_{k}")

    lnT_keys_all = [("lnT", i, k) for i in range(NTH) for k in range(KD)]

    def win_slice(par, o0):
        par = par % 2
        dst = wst[:, par * 1024:(par + 1) * 1024]
        for k in range(KD):
            p.dma("sp", dst[:, k * 128:(k + 1) * 128],
                  winscr[o0:o0 + 128, k * 128:(k + 1) * 128], reads=["winscr"],
                  writes=[("wst", par)], desc=f"wins{o0}_{k}", transpose=True)
        return dst

    for cg in range(16):
        par = cg % 2
        wsl = win_slice(par, cg * 128)
        for g5 in range(4):
            bank, bkey = (ps0, "ps0") if g5 % 2 == 0 else (ps1, "ps1")
            mm_chain(ctx, bank[:], bkey,
                     [(wsl[:, k * 128:(k + 1) * 128], [("wst", par)],
                       big_ln[:, k * TH + 128 + g5 * 512:k * TH + 128 + (g5 + 1) * 512],
                       lnT_keys_all) for k in range(KD)],
                     desc=f"gate{cg}_{g5}")
            dst = gateT[:, cg * 2048 + g5 * 512:cg * 2048 + (g5 + 1) * 512]
            p.op("act", lambda E, dst=dst, bank=bank: E.activation(
                out=dst, in_=bank[:], func=AF.Gelu),
                reads=[bkey], writes=[("gateT", cg)], desc=f"gelu{cg}_{g5}")

    for cg in range(16):
        par = cg % 2
        wsl = win_slice(par, 2048 + cg * 128)
        for g5 in range(5):
            w0 = g5 * 512
            w1 = min(2176, w0 + 512)
            bank, bkey = (ps0, "ps0") if g5 % 2 == 0 else (ps1, "ps1")
            mm_chain(ctx, bank[:, 0:w1 - w0], bkey,
                     [(wsl[:, k * 128:(k + 1) * 128], [("wst", par)],
                       big_ln[:, k * TH + w0:k * TH + w1],
                       lnT_keys_all) for k in range(KD)],
                     desc=f"xc{cg}_{g5}")
            dst = xcT[:, w0:w1]
            p.op("dve", lambda E, dst=dst, bank=bank, n=w1 - w0: E.tensor_copy(
                dst, bank[:, 0:n]),
                reads=[bkey], writes=[("xcT", g5)], desc=f"exc{cg}_{g5}")
        xck = [("xcT", g) for g in range(5)]
        p.op("dve", lambda E, cg=cg: E.tensor_scalar(
            sc1[:], xcT[:, 125:125 + 2048], cvw[:, cg * 4:cg * 4 + 1], None, ALU.mult),
            reads=xck, scalar_reads=["cvw"], writes=["sc1"], desc=f"cv0_{cg}")
        for k in range(1, 4):
            p.op("dve", lambda E, cg=cg, k=k: E.scalar_tensor_tensor(
                sc1[:], xcT[:, 125 + k:125 + k + 2048], cvw[:, cg * 4 + k:cg * 4 + k + 1],
                sc1[:], ALU.mult, ALU.add),
                reads=xck + ["sc1"], scalar_reads=["cvw"], writes=["sc1"],
                desc=f"cv{k}_{cg}")
        p.op("dve", lambda E, cg=cg: E.tensor_scalar(
            sc1[:], sc1[:], cvb[:, cg:cg + 1], None, ALU.add),
            reads=["sc1"], scalar_reads=["cvb"], writes=["sc1"], desc=f"cvb_{cg}")
        dst = bigB[:, cg * 2048:(cg + 1) * 2048]
        p.op("dve", lambda E, dst=dst: E.tensor_copy(dst, sc1[:]),
             reads=["sc1"], writes=[("xcc", cg)], desc=f"excc{cg}")

    p.barrier()
    p.op("pool", lambda E: E.memset(xcT[:, 0:2048], 0.0),
         writes=[("xcT", g) for g in range(5)] + ["zeros2k"], desc="z2k")
    hP_d = ctx.dram("hP_d", [2 * 2048, 2048], F32)
    xcc_keys = [("xcc", c) for c in range(16)]
    for cg in range(16):
        par = 0
        wg_f = wgt[:, 0:2048]
        wg_i = wgt[:, 2048:4096]
        for k in range(16):
            p.dma("sp", wg_f[:, k * 128:(k + 1) * 128],
                  wgscr[cg * 128:(cg + 1) * 128, k * 128:(k + 1) * 128],
                  reads=["wgscr"], writes=[("wgt", par, "f")],
                  desc=f"wgf{cg}_{k}", transpose=True)
            p.dma("sp", wg_i[:, k * 128:(k + 1) * 128],
                  wgscr[2048 + cg * 128:2048 + (cg + 1) * 128, k * 128:(k + 1) * 128],
                  reads=["wgscr"], writes=[("wgt", par, "i")],
                  desc=f"wgi{cg}_{k}", transpose=True)
        for part, wsl, wkey, bcol, dsts in (
                ("f", wg_f, ("wgt", par, "f"), cg, "sc4"),
                ("i", wg_i, ("wgt", par, "i"), 16 + cg, "sc2")):
            dstb = sc4 if part == "f" else sc2
            for g5 in range(4):
                bank, bkey = (ps0, "ps0") if g5 % 2 == 0 else (ps1, "ps1")
                mm_chain(ctx, bank[:], bkey,
                         [(wsl[:, k * 128:(k + 1) * 128], [wkey],
                           bigB[:, k * 2048 + g5 * 512:k * 2048 + (g5 + 1) * 512],
                           xcc_keys) for k in range(16)],
                         desc=f"g{part}{cg}_{g5}")
                dst = dstb[:, g5 * 512:(g5 + 1) * 512]
                p.op("act", lambda E, dst=dst, bank=bank, bcol=bcol: E.activation(
                    out=dst, in_=bank[:], func=AF.Sigmoid, bias=bgc[:, bcol:bcol + 1]),
                    reads=[bkey], scalar_reads=["bgc"], writes=[dsts],
                    desc=f"sg{part}{cg}_{g5}")
        p.op("dve", lambda E, cg=cg: E.tensor_scalar(
            sc1[:], sc4[:], afb[:, cg:cg + 1], None, ALU.mult),
            reads=["sc4"], scalar_reads=["afb"], writes=["sc1"], desc=f"la{cg}")
        p.op("act", lambda E: E.activation(out=sc4[:], in_=sc1[:], func=AF.Exp),
             reads=["sc1"], writes=["sc4"], desc=f"al{cg}")
        p.op("dve", lambda E: E.tensor_mul(sc1[:], sc4[:], sc4[:]),
             reads=["sc4"], writes=["sc1"], desc=f"a2{cg}")
        p.op("dve", lambda E: E.tensor_scalar(sc1[:], sc1[:], -1.0, 1.0 + 1e-6,
                                              ALU.mult, ALU.add),
             reads=["sc1"], writes=["sc1"], desc=f"b2{cg}")
        p.op("act", lambda E: E.activation(out=sc1[:], in_=sc1[:], func=AF.Sqrt),
             reads=["sc1"], writes=["sc1"], desc=f"bs{cg}")
        p.op("dve", lambda E: E.tensor_mul(sc2[:], sc2[:], sc1[:]),
             reads=["sc1", "sc2"], writes=["sc2"], desc=f"bt{cg}")
        xsl = bigB[:, cg * 2048:(cg + 1) * 2048]
        p.op("dve", lambda E, xsl=xsl: E.tensor_mul(sc2[:], sc2[:], xsl),
             reads=[("xcc", cg), "sc2"], writes=["sc2"], desc=f"bx{cg}")
        p.op("dve", lambda E: E.tensor_tensor_scan(sc3[:], sc4[:], sc2[:], 0.0,
                                                   ALU.mult, ALU.add),
             reads=["sc4", "sc2"], writes=["sc3"], desc=f"scanh{cg}")
        p.op("dve", lambda E: E.tensor_tensor_scan(sc2[:], sc4[:], zeros2k[:, 0:2048], 1.0,
                                                   ALU.mult, ALU.add),
             reads=["sc4", "zeros2k"], writes=["sc2"], desc=f"scanP{cg}")
        p.op("dve", lambda E, cg=cg: E.tensor_copy(carry[:, cg:cg + 1],
                                                   sc3[:, 2047:2048]),
             reads=["sc3"], writes=[("carry", cg)], desc=f"cr{cg}")
        p.dma("sp", hP_d[cg * 128:(cg + 1) * 128, :], sc3[:], reads=["sc3"],
              writes=[("hd", cg)], desc=f"sth{cg}")
        p.dma("sp", hP_d[2048 + cg * 128:2048 + (cg + 1) * 128, :], sc2[:],
              reads=["sc2"], writes=[("pd", cg)], desc=f"stP{cg}")

    car_i = ctx.dram("car_i", [2048, 1], F32)
    car_o = ctx.dram("car_o", [4096, 1], F32)
    p.dma("sp", car_i.rearrange("(c p) one -> p (c one)", p=128), carry[:],
          reads=[("carry", c) for c in range(16)], writes=["car_i"], desc="stcar",
          allow_slow_non_contiguous=True)
    groups = [[2 * g, 2 * g + 1] for g in range(4)]
    p.op("pool", lambda E: E.collective_compute(
        "AllGather", ALU.bypass, groups, [car_i.opt()], [car_o.opt()]),
        reads=["car_i"], writes=["car_o"], desc="ag_car")
    p.dma("sp", carin[:], car_o[0:2048].rearrange("(c p) one -> p (c one)", p=128),
          reads=["car_o"], writes=["carin"], desc="ldcar",
          allow_slow_non_contiguous=True)
    p.op("dve", lambda E: E.tensor_scalar(carin[:], carin[:], hmask[:], None, ALU.mult),
         reads=["carin"], scalar_reads=["hmask"], writes=["carin"], desc="mcar")

    p.barrier()
    for k in range(16):
        dst = wot[:, k * 1024:(k + 1) * 1024]
        p.dma("sp", dst, woscr[:, k * 128:(k + 1) * 128], reads=["woscr"],
              writes=[("wot", k)], desc=f"ldwoT{k}", transpose=True)
    for cg in range(16):
        p.dma("sp", sc3[:], hP_d[cg * 128:(cg + 1) * 128, :], reads=[("hd", cg)],
              writes=["sc3"], desc=f"ldh{cg}")
        p.dma("sp", sc2[:], hP_d[2048 + cg * 128:2048 + (cg + 1) * 128, :],
              reads=[("pd", cg)], writes=["sc2"], desc=f"ldP{cg}")
        p.op("dve", lambda E, cg=cg: E.scalar_tensor_tensor(
            sc3[:], sc2[:], carin[:, cg:cg + 1], sc3[:], ALU.mult, ALU.add),
            reads=["sc2", "sc3"], scalar_reads=["carin"], writes=["sc3"],
            desc=f"fx{cg}")
        gsl = gateT[:, cg * 2048:(cg + 1) * 2048]
        zdst = bigB[:, cg * 2048:(cg + 1) * 2048]
        p.op("dve", lambda E, gsl=gsl, zdst=zdst: E.tensor_mul(zdst, gsl, sc3[:]),
             reads=[("gateT", cg), "sc3"], writes=[("z", cg)], desc=f"z{cg}")

    p.barrier()
    z_keys = [("z", c) for c in range(16)]
    for t in range(NT):
        p.dma("sp", xtile[:], xo_t[t], writes=[("x", 0)], desc=f"ldxo{t}")
        for hf in range(2):
            bank, bkey = (ps2, "ps2") if hf == 0 else (ps3, "ps3")
            mm_chain(ctx, bank[:], bkey,
                     [(bigB[:, k * 2048 + t * 128:k * 2048 + (t + 1) * 128], z_keys,
                       wot[:, k * 1024 + hf * 512:k * 1024 + hf * 512 + 512],
                       [("wot", k)]) for k in range(16)],
                     desc=f"y{t}_{hf}")
            dst = xtile[:, hf * 512:(hf + 1) * 512]
            p.op("dve", lambda E, dst=dst, bank=bank: E.tensor_add(dst, dst, bank[:]),
                 reads=[bkey, ("x", 0)], writes=[("x", 0)], desc=f"res{t}_{hf}")
        p.dma("sp", x_out_t[t], xtile[:], reads=[("x", 0)], writes=[("xo", t)],
              desc=f"stx{t}")


def build_L3_attn(ctx, core, xb_in, xown_in, x_out, W):
    nc, p = ctx.nc, ctx.p
    ident = make_identity(ctx)
    TB = 4096
    NTB = 32
    NL = 64
    WIN = 256

    big1 = ctx.sb("big1", [128, KD * TB], BF16)
    kfm = ctx.sb("kfm", [128, 4 * TB], BF16)
    qfm = ctx.sb("qfm", [128, 4 * TB], BF16)
    v2 = ctx.sb("v2", [128, NTB * 520], BF16)
    pt = ctx.sb("pt", [128, 4 * 256], BF16)
    lsb = ctx.sb("lsb", [128, 512], BF16)
    masks = ctx.sb("masks", [128, 4 * 256], BF16)
    ptl = None
    lmthr = ctx.sb("lmthr", [128, 4], F32)
    glc = ctx.sb("glc", [128, 512], F32)
    xpT = ctx.sb("xpT", [128, 8 * 64], BF16)
    gkv = ctx.sb("gkv", [128, 8 * 64 + 520], BF16)
    poolw = None
    lnt = ctx.sb("lnt", [128, D], BF16)
    xtile = ctx.sb("xtile", [128, D], F32)
    lnscr = ctx.sb("lnscr", [128, D], F32)
    stat = ctx.sb("stat", [128, 8], F32)
    rcol = ctx.sb("rcol", [128, 8], F32)
    wst = ctx.sb("wst", [128, 3 * 1024], BF16)
    wg3 = ctx.sb("wg3", [128, 8 * 512], BF16)
    wrow = ctx.sb("wrow", [128, 4 * 512], BF16)
    wout = ctx.sb("wout", [128, 4 * 1024], BF16)
    glw = ctx.sb("glwc", [128, 16], BF16)
    ps0 = ctx.ps("ps0", [128, 512], F32)
    ps1 = ctx.ps("ps1", [128, 512], F32)
    ps2 = ctx.ps("ps2", [128, 512], F32)
    ps3 = ctx.ps("ps3", [128, 512], F32)
    ps4 = ctx.ps("ps4", [128, 512], F32)
    ps_tr = ctx.ps("ps_tr", [128, 1024], BF16)

    xb_t = xb_in.rearrange("(i p) d -> i p d", p=128)
    xo_t = xown_in.rearrange("(i p) d -> i p d", p=128)
    x_out_t = x_out.rearrange("(i p) d -> i p d", p=128)
    poolw = qfm[:, 0:2048]

    localTok = big1[:, 0:16384]
    remTok = big1[:, 16384:32768]
    outT = kfm

    for m in range(4):
        msl = masks[:, m * 256:(m + 1) * 256]
        p.op("pool", lambda E, msl=msl: E.memset(msl, 1.0), writes=[("masks", m)], desc=f"mz{m}")
        p.op("pool", lambda E, msl=msl, m=m: E.affine_select(
            out=msl, in_=msl, compare_op=ALU.is_ge, fill=0.0,
            base=256 - m * 128, pattern=[[1, 256]], channel_multiplier=-1),
            reads=[("masks", m)], writes=[("masks", m)], desc=f"m1{m}")
        p.op("pool", lambda E, msl=msl, m=m: E.affine_select(
            out=msl, in_=msl, compare_op=ALU.is_ge, fill=0.0,
            base=m * 128 - 1, pattern=[[-1, 256]], channel_multiplier=1),
            reads=[("masks", m)], writes=[("masks", m)], desc=f"m2{m}")
    p.op("pool", lambda E: E.iota(lmthr[:, 0:1], pattern=[[0, 1]], base=0,
                                  channel_multiplier=64,
                                  allow_small_or_imprecise_dtypes=True),
         writes=["lmthr"], desc="lmthr")
    qprow = pt[:, 0:512]
    ptl = pt[:, 512:1024]
    p.op("pool", lambda E: E.iota(qprow[:], pattern=[[1, 512]], base=0,
                                  channel_multiplier=0,
                                  allow_small_or_imprecise_dtypes=True),
         writes=["qprow"], desc="qprow")
    for t in range(NTB):
        sl = poolw[:, t * 64:(t + 1) * 64]
        p.op("pool", lambda E, sl=sl: E.memset(sl, 1.0 / 64.0), writes=[("poolw", t)], desc=f"pz{t}")
        p.op("pool", lambda E, sl=sl, t=t: E.affine_select(
            out=sl, in_=sl, compare_op=ALU.is_ge, fill=0.0,
            base=128 * t, pattern=[[-64, 64]], channel_multiplier=1),
            reads=[("poolw", t)], writes=[("poolw", t)], desc=f"p1{t}")
        p.op("pool", lambda E, sl=sl, t=t: E.affine_select(
            out=sl, in_=sl, compare_op=ALU.is_ge, fill=0.0,
            base=63 - 128 * t, pattern=[[64, 64]], channel_multiplier=-1),
            reads=[("poolw", t)], writes=[("poolw", t)], desc=f"p2{t}")
    p.dma("pool", glw[:, 0:4], W["glw"].rearrange("(c two) d -> (two d) c", two=2),
          writes=["glw"], desc="glw", allow_slow_non_contiguous=True)
    p.dma("pool", glw[:, 8:12], W["grw"].rearrange("(c two) d -> (two d) c", two=2),
          writes=["glw"], desc="grw", allow_slow_non_contiguous=True)
    src = W["r_out_w"].rearrange("(k p) n -> p k n", p=128)
    p.dma("pool", wrow[:].rearrange("p (k n) -> p k n", n=512), src,
          writes=["wrow"], desc="ldwrow")
    src = W["w_out"].rearrange("(k p) n -> p k n", p=128)
    p.dma("pool", wout[:].rearrange("p (k n) -> p k n", n=1024), src,
          writes=["wout"], desc="ldwout")

    for i in range(NTB):
        p.dma("sp", xtile[:], xb_t[i], writes=[("x", 0)], desc=f"ldx{i}")
        layernorm_tile(ctx, lnt[:], xtile[:], lnscr, stat, i, src_key=("x", 0),
                       dst_key=("ln", 0))
        for k in range(KD):
            transpose_128(ctx, ps_tr, "ps_tr", k, lnt[:, k * 128:(k + 1) * 128],
                          ident, reads=[("ln", 0)])
        for k in range(KD):
            dst = big1[:, k * TB + i * 128:k * TB + (i + 1) * 128]
            p.op("dve", lambda E, dst=dst, k=k: E.tensor_copy(
                dst, ps_tr[:, k * 128:(k + 1) * 128]),
                reads=["ps_tr"], writes=[("lnT", i, k)], desc=f"elnT{i}_{k}")
        for hf in range(2):
            bank, bkey = (ps3, "ps3") if hf == 0 else (ps4, "ps4")
            p.op("pe", lambda E, bank=bank, i=i, hf=hf: E.matmul(
                bank[0:64, :], poolw[:, i * 64:(i + 1) * 64],
                lnt[:, hf * 512:(hf + 1) * 512], start=(i == 0), stop=(i == NTB - 1)),
                reads=[("poolw", i), ("ln", 0)], writes=[bkey], desc=f"xp{i}_{hf}")
    xp_sb = lnt
    for hf in range(2):
        bank, bkey = (ps3, "ps3") if hf == 0 else (ps4, "ps4")
        p.op("dve", lambda E, bank=bank, hf=hf: E.tensor_copy(
            xp_sb[0:64, hf * 512:(hf + 1) * 512], bank[0:64, :]),
            reads=[bkey], writes=[("xp", hf)], desc=f"exp{hf}")
    for k in range(KD):
        p.op("pe", lambda E, k=k: E.transpose(
            ps_tr[:, k * 64:(k + 1) * 64], xp_sb[0:64, k * 128:(k + 1) * 128],
            ident[0:64, 0:64]),
            reads=[("xp", 0), ("xp", 1), "ident"], writes=["ps_tr"], desc=f"trxp{k}")
    p.op("dve", lambda E: E.tensor_copy(xpT[:, 0:512], ps_tr[:, 0:512]),
         reads=["ps_tr"], writes=["xpT"], desc="expT")

    lnT_keys_all = [("lnT", i, k) for i in range(NTB) for k in range(KD)]

    def lnT_chunk(k, t0, n):
        return big1[:, k * TB + t0:k * TB + t0 + n]

    for pp in range(4):
        par = 0
        for wi, hb in ((0, W["wq"]), (1, W["wk"]), (2, W["wv"])):
            dst = wst[:, par * 3072 + wi * 1024:par * 3072 + wi * 1024 + 1024]
            p.dma("pool", dst.rearrange("p (k n) -> p k n", n=128),
                  hb.rearrange("(k p) n -> p k n", p=128)[:, :, pp * 128:(pp + 1) * 128],
                  writes=[("wst", par, wi)], desc=f"ldw{wi}_{pp}")
        wq_s = wst[:, par * 3072:par * 3072 + 1024]
        wk_s = wst[:, par * 3072 + 1024:par * 3072 + 2048]
        wv_s = wst[:, par * 3072 + 2048:par * 3072 + 3072]
        for gqi in range(8):
            for hh in range(2):
                bank, bkey = (ps0, "ps0") if hh == 0 else (ps1, "ps1")
                mm_chain(ctx, bank[hh * 64:hh * 64 + 64, :], bkey,
                         [(wq_s[:, k * 128 + hh * 64:k * 128 + hh * 64 + 64],
                           [("wst", par, 0)],
                           lnT_chunk(k, gqi * 512, 512), lnT_keys_all) for k in range(KD)],
                         desc=f"q{pp}_{gqi}_{hh}")
                dst = qfm[hh * 64:hh * 64 + 64, pp * TB + gqi * 512:pp * TB + (gqi + 1) * 512]
                p.op("dve", lambda E, dst=dst, bank=bank, hh=hh: E.tensor_copy(
                    dst, bank[hh * 64:hh * 64 + 64, :]),
                    reads=[bkey], writes=[("qfm", pp, gqi)], desc=f"eq{pp}_{gqi}_{hh}")
                bank2, bkey2 = (ps2, "ps2") if hh == 0 else (ps3, "ps3")
                mm_chain(ctx, bank2[hh * 64:hh * 64 + 64, :], bkey2,
                         [(wk_s[:, k * 128 + hh * 64:k * 128 + hh * 64 + 64],
                           [("wst", par, 1)],
                           lnT_chunk(k, gqi * 512, 512), lnT_keys_all) for k in range(KD)],
                         desc=f"k{pp}_{gqi}_{hh}")
                dst = kfm[hh * 64:hh * 64 + 64, pp * TB + gqi * 512:pp * TB + (gqi + 1) * 512]
                p.op("dve", lambda E, dst=dst, bank=bank2, hh=hh: E.tensor_scalar(
                    dst, bank[hh * 64:hh * 64 + 64, :] if False else bank2[hh * 64:hh * 64 + 64, :],
                    0.125, None, ALU.mult),
                    reads=[bkey2], writes=[("kfm", pp, gqi)], desc=f"ek{pp}_{gqi}_{hh}")
        for t in range(NTB):
            for hh in range(2):
                h = 2 * pp + hh
                bank, bkey = (ps0, "ps0") if hh == 0 else (ps1, "ps1")
                mm_chain(ctx, bank[:, 0:64], bkey,
                         [(lnT_chunk(k, t * 128, 128), lnT_keys_all,
                           wv_s[:, k * 128 + hh * 64:k * 128 + hh * 64 + 64],
                           [("wst", par, 2)]) for k in range(KD)],
                         desc=f"v{pp}_{t}_{hh}")
                dst = v2[:, t * 520 + h * 65:t * 520 + h * 65 + 64]
                p.op("dve", lambda E, dst=dst, bank=bank: E.tensor_copy(dst, bank[:, 0:64]),
                     reads=[bkey], writes=[("v2", t)], desc=f"ev{pp}_{t}_{hh}")
    for t in range(NTB):
        p.op("pool", lambda E, t=t: E.memset(v2[:, t * 520 + 64:t * 520 + 65], 1.0),
             writes=[("v2", t)], desc=f"vo{t}")
        for h in range(1, 8):
            p.op("pool", lambda E, t=t, h=h: E.tensor_copy(
                v2[:, t * 520 + h * 65 + 64:t * 520 + h * 65 + 65],
                v2[:, t * 520 + 64:t * 520 + 65]),
                reads=[("v2", t)], writes=[("v2", t)], desc=f"vo{t}_{h}")

    gkT = gkv[:, 0:256]
    gvT = gkv[:, 256:776]
    src = W["wgk"].rearrange("(k p) n -> p k n", p=128)
    p.dma("pool", wg3[:].rearrange("p (k n) -> p k n", n=512), src,
          writes=[("wg3", "k")], desc="ldwgk")
    for pp in range(4):
        for hh in range(2):
            bank, bkey = (ps0, "ps0") if hh == 0 else (ps1, "ps1")
            mm_chain(ctx, bank[hh * 64:hh * 64 + 64, 0:64], bkey,
                     [(wg3[:, k * 512 + pp * 128 + hh * 64:k * 512 + pp * 128 + hh * 64 + 64],
                       [("wg3", "k")],
                       xpT[:, k * 64:(k + 1) * 64], ["xpT"]) for k in range(KD)],
                     desc=f"gk{pp}_{hh}")
            dst = gkT[hh * 64:hh * 64 + 64, pp * 64:(pp + 1) * 64]
            p.op("dve", lambda E, dst=dst, bank=bank, hh=hh: E.tensor_scalar(
                dst, bank[hh * 64:hh * 64 + 64, 0:64], 0.125, None, ALU.mult),
                reads=[bkey], writes=[("gkT", pp)], desc=f"egk{pp}_{hh}")
    src = W["wgv"].rearrange("(k p) n -> p k n", p=128)
    p.dma("pool", wg3[:].rearrange("p (k n) -> p k n", n=512), src,
          writes=[("wg3", "v")], desc="ldwgv")
    for lh in range(8):
        bank, bkey = (ps0, "ps0") if lh % 2 == 0 else (ps1, "ps1")
        mm_chain(ctx, bank[0:64, 0:64], bkey,
                 [(xpT[:, k * 64:(k + 1) * 64], ["xpT"],
                   wg3[:, k * 512 + lh * 64:k * 512 + (lh + 1) * 64], [("wg3", "v")])
                  for k in range(KD)],
                 desc=f"gv{lh}")
        dst = gvT[0:64, lh * 65:lh * 65 + 64]
        p.op("dve", lambda E, dst=dst, bank=bank: E.tensor_copy(dst, bank[0:64, 0:64]),
             reads=[bkey], writes=[("gvT", lh)], desc=f"egv{lh}")
        p.op("pool", lambda E, lh=lh: E.memset(gvT[0:64, lh * 65 + 64:lh * 65 + 65], 1.0),
             writes=[("gvT", lh)], desc=f"gvo{lh}")
    src = W["wgq"].rearrange("(k p) n -> p k n", p=128)
    p.dma("pool", wg3[:].rearrange("p (k n) -> p k n", n=512), src,
          writes=[("wg3", "q")], desc="ldwgq")
    gqAll_d = ctx.dram("gqAll_d", [512, TB], BF16)
    for pp in range(4):
        for gqi in range(8):
            for hh in range(2):
                bank, bkey = (ps0, "ps0") if hh == 0 else (ps1, "ps1")
                mm_chain(ctx, bank[hh * 64:hh * 64 + 64, :], bkey,
                         [(wg3[:, k * 512 + pp * 128 + hh * 64:k * 512 + pp * 128 + hh * 64 + 64],
                           [("wg3", "q")],
                           big1[:, k * TB + gqi * 512:k * TB + (gqi + 1) * 512],
                           lnT_keys_all) for k in range(KD)],
                         desc=f"gq{pp}_{gqi}_{hh}")
                dst = lsb[0:64, 0:512] if hh == 0 else lsb[64:128, 0:512]
                p.op("dve", lambda E, dst=dst, bank=bank, hh=hh: E.tensor_copy(
                    dst, bank[hh * 64:hh * 64 + 64, :]),
                    reads=[bkey], writes=[("lsb", hh)], desc=f"egq{pp}_{gqi}_{hh}")
            p.dma("sp", gqAll_d[pp * 128:(pp + 1) * 128, gqi * 512:(gqi + 1) * 512],
                  lsb[:, 0:512], reads=[("lsb", 0), ("lsb", 1)],
                  writes=[("gqd", pp, gqi)], desc=f"stgq{pp}_{gqi}")

    p.barrier()
    gqAll = big1[:, 0:16384]
    for pp in range(4):
        p.dma("sp", gqAll.rearrange("p (a n) -> p a n", a=4)[:, pp, :],
              gqAll_d[pp * 128:(pp + 1) * 128, :],
              reads=[("gqd", pp, g) for g in range(8)], writes=[("gqA", pp)],
              desc=f"ldgq{pp}")

    for pp in range(4):
        for hh in range(2):
            lh = 2 * pp + hh
            for gqi in range(8):
                sbank, sbkey = (ps0, "ps0") if gqi % 2 == 0 else (ps1, "ps1")
                p.op("pe", lambda E, b=sbank, pp=pp, hh=hh, gqi=gqi: E.matmul(
                    b[0:64, :], gkT[hh * 64:hh * 64 + 64, pp * 64:(pp + 1) * 64],
                    gqAll[hh * 64:hh * 64 + 64, pp * 4096 + gqi * 512:pp * 4096 + (gqi + 1) * 512],
                    start=True, stop=True),
                    reads=[("gkT", pp), ("gqA", pp)], writes=[sbkey],
                    desc=f"lsc{lh}_{gqi}")
                p.op("act", lambda E, b=sbank: E.activation(
                    out=lsb[0:64, 0:512], in_=b[0:64, :], func=AF.Exp),
                    reads=[sbkey], writes=["lsb"], desc=f"lex{lh}_{gqi}")
                p.op("dve", lambda E, gqi=gqi: E.tensor_scalar(
                    rcol[:, 4:5], lmthr[:, 0:1], float(-gqi * 512), None, ALU.add),
                    reads=["lmthr"], writes=[("rcol", "thr")], desc=f"lth{lh}_{gqi}")
                p.op("dve", lambda E: E.scalar_tensor_tensor(
                    ptl[0:64, :], qprow[0:64, :], rcol[0:64, 4:5], lsb[0:64, 0:512],
                    ALU.is_ge, ALU.mult),
                    reads=["qprow", "lsb"], scalar_reads=[("rcol", "thr")],
                    writes=["ptl"], desc=f"lpt{lh}_{gqi}")
                pbank, pbkey = (ps2, "ps2") if gqi % 2 == 0 else (ps3, "ps3")
                p.op("pe", lambda E, b=pbank, lh=lh: E.matmul(
                    b[0:65, :], gvT[0:64, lh * 65:(lh + 1) * 65], ptl[0:64, :],
                    start=True, stop=True),
                    reads=[("gvT", lh), "ptl"], writes=[pbkey], desc=f"lpv{lh}_{gqi}")
                p.op("dve", lambda E, b=pbank: E.tensor_copy(lsb[0:65, 0:512], b[0:65, :]),
                     reads=[pbkey], writes=["lsb"], desc=f"lev{lh}_{gqi}")
                for piece in range(4):
                    p.op("pe", lambda E, piece=piece: E.transpose(
                        ps_tr[:, piece * 66:piece * 66 + 65],
                        lsb[0:65, piece * 128:(piece + 1) * 128], ident[0:65, 0:65]),
                        reads=["lsb", "ident"], writes=["ps_tr"],
                        desc=f"ltr{lh}_{gqi}_{piece}")
                for piece in range(4):
                    t = gqi * 4 + piece
                    p.op("dve", lambda E, piece=piece: E.reciprocal(
                        rcol[:, 5:6], ps_tr[:, piece * 66 + 64:piece * 66 + 65]),
                        reads=["ps_tr"], writes=[("rcol", "rc")],
                        desc=f"lrc{lh}_{gqi}_{piece}")
                    dst = big1[:, 16384 + t * 512 + lh * 64:16384 + t * 512 + (lh + 1) * 64]
                    p.op("dve", lambda E, dst=dst, piece=piece: E.tensor_scalar(
                        dst, ps_tr[:, piece * 66:piece * 66 + 64], rcol[:, 5:6],
                        None, ALU.mult),
                        reads=["ps_tr"], scalar_reads=[("rcol", "rc")],
                        writes=[("remTok", t)], desc=f"lsc2{lh}_{gqi}_{piece}")

    p.barrier()
    for h in range(8):
        pp, hh = h // 2, h % 2
        for n in range(16):
            kcs = [2, 3] if n == 0 else [0, 1, 2, 3]
            for kc in kcs:
                kt = 2 * n - 2 + kc
                sbank, sbkey = (ps0, "ps0") if kc % 2 == 0 else (ps1, "ps1")
                lt_ = kfm[hh * 64:hh * 64 + 64, pp * TB + kt * 128:pp * TB + (kt + 1) * 128]
                rh_ = qfm[hh * 64:hh * 64 + 64, pp * TB + n * 256:pp * TB + n * 256 + 256]
                p.op("pe", lambda E, b=sbank, lt_=lt_, rh_=rh_: E.matmul(
                    b[:, 0:256], lt_, rh_, start=True, stop=True),
                    reads=[("kfm", pp, kt // 4), ("qfm", pp, (n * 256) // 512)],
                    writes=[sbkey], desc=f"sc{h}_{n}_{kc}")
                ptb = pt[:, kc * 256:(kc + 1) * 256]
                p.op("act", lambda E, b=sbank, ptb=ptb: E.activation(
                    out=ptb, in_=b[:, 0:256], func=AF.Exp),
                    reads=[sbkey], writes=[("pt", kc)], desc=f"ex{h}_{n}_{kc}")
                msl = masks[:, kc * 256:(kc + 1) * 256]
                p.op("dve", lambda E, ptb=ptb, msl=msl: E.tensor_mul(ptb, ptb, msl),
                     reads=[("pt", kc), ("masks", kc)], writes=[("pt", kc)],
                     desc=f"mk{h}_{n}_{kc}")
            pbank, pbkey = (ps2, "ps2") if n % 2 == 0 else (ps3, "ps3")
            mm_chain(ctx, pbank[0:65, 0:256], pbkey,
                     [(v2[:, (2 * n - 2 + kc) * 520 + h * 65:(2 * n - 2 + kc) * 520 + (h + 1) * 65],
                       [("v2", 2 * n - 2 + kc)],
                       pt[:, kc * 256:(kc + 1) * 256], [("pt", kc)]) for kc in kcs],
                     desc=f"pv{h}_{n}")
            p.op("dve", lambda E, b=pbank: E.tensor_copy(lsb[0:65, 0:256], b[0:65, 0:256]),
                 reads=[pbkey], writes=["lsb"], desc=f"lve{h}_{n}")
            for piece in range(2):
                p.op("pe", lambda E, piece=piece: E.transpose(
                    ps_tr[:, piece * 66:piece * 66 + 65],
                    lsb[0:65, piece * 128:(piece + 1) * 128], ident[0:65, 0:65]),
                    reads=["lsb", "ident"], writes=["ps_tr"], desc=f"tr{h}_{n}_{piece}")
            for piece in range(2):
                t = 2 * n + piece
                p.op("dve", lambda E, piece=piece: E.reciprocal(
                    rcol[:, 6:7], ps_tr[:, piece * 66 + 64:piece * 66 + 65]),
                    reads=["ps_tr"], writes=[("rcol", "rcl")], desc=f"rc{h}_{n}_{piece}")
                dst = big1[:, t * 512 + h * 64:t * 512 + (h + 1) * 64]
                p.op("dve", lambda E, dst=dst, piece=piece: E.tensor_scalar(
                    dst, ps_tr[:, piece * 66:piece * 66 + 64], rcol[:, 6:7],
                    None, ALU.mult),
                    reads=["ps_tr"], scalar_reads=[("rcol", "rcl")],
                    writes=[("localTok", t)], desc=f"le{h}_{n}_{piece}")

    for h in range(8):
        pp, hh = h // 2, h % 2
        for t in range(NTB):
            for gi, (c0, nm) in enumerate(((0, "l"), (8, "r"))):
                bank, bkey = (ps4, "ps4")
                p.op("pe", lambda E, pp=pp, hh=hh, t=t, c0=c0: E.matmul(
                    ps4[:, 0:1],
                    qfm[hh * 64:hh * 64 + 64, pp * TB + t * 128:pp * TB + (t + 1) * 128],
                    glw[hh * 64:hh * 64 + 64, c0 + pp:c0 + pp + 1],
                    start=True, stop=True),
                    reads=[("qfm", pp, t // 4), "glw"], writes=["ps4"],
                    desc=f"g{nm}{h}_{t}")
                dst = glc[:, gi * 256 + h * 32 + t:gi * 256 + h * 32 + t + 1]
                p.op("act", lambda E, dst=dst: E.activation(
                    out=dst, in_=ps4[:, 0:1], func=AF.Sigmoid),
                    reads=["ps4"], writes=[("glc", gi, h, t)], desc=f"eg{nm}{h}_{t}")

    p.barrier()
    remcatT = kfm
    ctok = wg3[:, 0:512]
    for t in range(NTB):
        for ch in range(4):
            transpose_128(ctx, ps_tr, "ps_tr", ch,
                          big1[:, 16384 + t * 512 + ch * 128:16384 + t * 512 + (ch + 1) * 128],
                          ident, reads=[("remTok", t)])
        for ch in range(4):
            dst = remcatT[:, ch * TB + t * 128:ch * TB + (t + 1) * 128]
            p.op("dve", lambda E, dst=dst, ch=ch: E.tensor_copy(
                dst, ps_tr[:, ch * 128:(ch + 1) * 128]),
                reads=["ps_tr"], writes=[("rcT", t, ch)], desc=f"ercT{t}_{ch}")
    for t in range(NTB):
        bank, bkey = (ps0, "ps0") if t % 2 == 0 else (ps1, "ps1")
        mm_chain(ctx, bank[:], bkey,
                 [(remcatT[:, ch * TB + t * 128:ch * TB + (t + 1) * 128],
                   [("rcT", t, ch)],
                   wrow[:, ch * 512:(ch + 1) * 512], ["wrow"]) for ch in range(4)],
                 desc=f"rem{t}")
        p.op("dve", lambda E, bank=bank: E.tensor_copy(lsb[:, 0:512], bank[:]),
             reads=[bkey], writes=["lsb"], desc=f"erem{t}")
        for h in range(8):
            dst = ctok[:, h * 64:(h + 1) * 64]
            lsl = big1[:, t * 512 + h * 64:t * 512 + (h + 1) * 64]
            p.op("dve", lambda E, dst=dst, lsl=lsl, h=h, t=t: E.tensor_scalar(
                dst, lsl, glc[:, h * 32 + t:h * 32 + t + 1], None, ALU.mult),
                reads=[("localTok", t)], scalar_reads=[("glc", 0, h, t)],
                writes=[("ctok", h)], desc=f"cb{t}_{h}")
            p.op("dve", lambda E, dst=dst, h=h, t=t: E.scalar_tensor_tensor(
                dst, lsb[:, h * 64:(h + 1) * 64],
                glc[:, 256 + h * 32 + t:256 + h * 32 + t + 1], dst, ALU.mult, ALU.add),
                reads=["lsb", ("ctok", h)], scalar_reads=[("glc", 1, h, t)],
                writes=[("ctok", h)], desc=f"cb2{t}_{h}")
        for ch in range(4):
            transpose_128(ctx, ps_tr, "ps_tr", ch, ctok[:, ch * 128:(ch + 1) * 128],
                          ident, reads=[("ctok", h2) for h2 in range(8)])
        for ch in range(4):
            dst = big1[:, 16384 + ch * TB + t * 128:16384 + ch * TB + (t + 1) * 128]
            p.op("dve", lambda E, dst=dst, ch=ch: E.tensor_copy(
                dst, ps_tr[:, ch * 128:(ch + 1) * 128]),
                reads=["ps_tr"], writes=[("outT", t, ch)], desc=f"eoT{t}_{ch}")

    yp_d = ctx.dram("yp_d", [TB, D], F32)
    ys_d = ctx.dram("ys_d", [TLOC, D], F32)
    yp_t = yp_d.rearrange("(i p) d -> i p d", p=128)
    ys_t = ys_d.rearrange("(i p) d -> i p d", p=128)
    yout = lnscr
    for t in range(NTB):
        for hf in range(2):
            bank, bkey = (ps2, "ps2") if hf == 0 else (ps3, "ps3")
            mm_chain(ctx, bank[:], bkey,
                     [(big1[:, 16384 + ch * TB + t * 128:16384 + ch * TB + (t + 1) * 128],
                       [("outT", t, ch)],
                       wout[:, ch * 1024 + hf * 512:ch * 1024 + hf * 512 + 512],
                       ["wout"]) for ch in range(4)],
                     desc=f"yo{t}_{hf}")
            p.op("dve", lambda E, bank=bank, hf=hf: E.tensor_copy(
                yout[:, hf * 512:(hf + 1) * 512], bank[:]),
                reads=[bkey], writes=[("yout", hf)], desc=f"eyo{t}_{hf}")
        p.dma("sp", yp_t[t], yout[:], reads=[("yout", 0), ("yout", 1)],
              writes=[("yp", t)], desc=f"styp{t}")
    groups = [[2 * g, 2 * g + 1] for g in range(4)]
    p.op("pool", lambda E: E.collective_compute(
        "ReduceScatter", ALU.add, groups, [yp_d.opt()], [ys_d.opt()]),
        reads=[("yp", t) for t in range(NTB)], writes=["ys"], desc="rs")
    for i in range(NT):
        p.dma("sp", xtile[:], xo_t[i], writes=[("x", 0)], desc=f"ldxo{i}")
        p.dma("sp", lnscr[:], ys_t[i], reads=["ys"], writes=["ybuf"], desc=f"ldys{i}")
        p.op("dve", lambda E: E.tensor_add(xtile[:], xtile[:], lnscr[:]),
             reads=["ybuf", ("x", 0)], writes=[("x", 0)], desc=f"r1{i}")
        p.dma("sp", x_out_t[i], xtile[:], reads=[("x", 0)], writes=[("xo", i)],
              desc=f"stx{i}")


def build_L3_ffn(ctx, core, x_in, x_out, W, dbg=None):
    nc, p = ctx.nc, ctx.p
    ident = make_identity(ctx)
    NGT = NTOK // 128

    xTo = ctx.sb("xTo", [128, NT * 1024], BF16)
    lnt = ctx.sb("lnt", [128, D], BF16)
    xtile = ctx.sb("xtile", [128, D], F32)
    lnscr = ctx.sb("lnscr", [128, D], F32)
    stat = ctx.sb("stat", [128, 8], F32)
    coef = ctx.sb("coef", [128, NGT * 8], F32)
    rsc = ctx.sb("rsc", [128, 512], F32)
    gbB = ctx.sb("gbB", [128, 64], F32)
    gwT = ctx.sb("gwT", [128, 8 * 64], BF16)
    xts = ctx.sb("xts", [128, 2 * 1024], BF16)
    wgT = ctx.sb("wgT", [128, 2 * 8 * 704], BF16)
    wuT = ctx.sb("wuT", [128, 2 * 8 * 704], BF16)
    wdT = ctx.sb("wdT", [128, 6 * 1024], BF16)
    sgT = ctx.sb("sgT", [128, 8 * 1408], BF16)
    suT = ctx.sb("suT", [128, 8 * 1408], BF16)
    sdT = ctx.sb("sdT", [128, 11 * 1024], BF16)
    htile = ctx.sb("htile", [128, 1536], BF16)
    hT = ctx.sb("hT", [128, 1536], BF16)
    silu_s = ctx.sb("silu_s", [128, 512], F32)
    ysb = ctx.sb("ysb", [128, 1024], F32)
    onesf = ctx.sb("onesf", [128, 128], F32)
    ps0 = ctx.ps("ps0", [128, 512], F32)
    ps1 = ctx.ps("ps1", [128, 512], F32)
    ps2 = ctx.ps("ps2", [128, 512], F32)
    ps3 = ctx.ps("ps3", [128, 512], F32)
    ps4 = ctx.ps("ps4", [128, 512], F32)
    ps5 = ctx.ps("ps5", [128, 512], F32)
    ps_tr = ctx.ps("ps_tr", [128, 1024], BF16)

    x_in_t = x_in.rearrange("(i p) d -> i p d", p=128)
    x_out_t = x_out.rearrange("(i p) d -> i p d", p=128)

    p.op("pool", lambda E: E.memset(onesf[:], 1.0), writes=["onesf"], desc="ones")
    identf = ctx.sb("identf", [128, 128], F32)
    p.op("pool", lambda E: E.memset(identf[:], 1.0), writes=["identf"], desc="idf0")
    p.op("pool", lambda E: E.affine_select(
        out=identf[:], in_=identf[:], compare_op=ALU.is_equal, fill=0.0,
        base=0, pattern=[[-1, 128]], channel_multiplier=1),
        reads=["identf"], writes=["identf"], desc="idf1")
    p.op("pool", lambda E: E.memset(htile[:, 704:768], 0.0), writes=[("htile", "pad")],
         desc="hpad")

    gwscr, _ = cast_weight_scratch(ctx, "gwscr", W["gate_w"], 64, 1024)
    for k in range(KD):
        p.dma("sp", gwT[:, k * 64:(k + 1) * 64], gwscr[0:64, k * 128:(k + 1) * 128],
              reads=["gwscr"], writes=["gwT"], desc=f"gwT{k}", transpose=True)
    sgscr, _ = cast_weight_scratch(ctx, "sgscr", W["Sg"], 1408, 1024)
    suscr, _ = cast_weight_scratch(ctx, "suscr", W["Su"], 1408, 1024)
    sdscr, _ = cast_weight_scratch(ctx, "sdscr", W["Sd"], 1024, 1408)
    for k in range(KD):
        p.dma("sp", sgT[:, k * 1408:(k + 1) * 1408],
              sgscr[:, k * 128:(k + 1) * 128], reads=["sgscr"], writes=["sgT"],
              desc=f"sgT{k}", transpose=True)
        p.dma("sp", suT[:, k * 1408:(k + 1) * 1408],
              suscr[:, k * 128:(k + 1) * 128], reads=["suscr"], writes=["suT"],
              desc=f"suT{k}", transpose=True)
    for k in range(11):
        p.dma("sp", sdT[:, k * 1024:(k + 1) * 1024],
              sdscr[:, k * 128:(k + 1) * 128], reads=["sdscr"], writes=["sdT"],
              desc=f"sdT{k}", transpose=True)
    gbrow = rsc[:, 192:256]
    p.dma("sp", gbrow[0:1, :], W["gate_bias"][None, :], writes=[("rsc", "gb")],
          desc="gbrow")
    p.op("pe", lambda E: E.matmul(ps0[:, 0:64], onesf[0:1, :], gbrow[0:1, :],
                                  start=True, stop=True),
         reads=[("rsc", "gb"), "onesf"], writes=["ps0"], desc="gb_mm")
    p.op("dve", lambda E: E.tensor_copy(gbB[:], ps0[:, 0:64]),
         reads=["ps0"], writes=["gbB"], desc="gb_e")

    ag_i = ctx.dram("ag_i", [TLOC, D], BF16)
    ag_o = ctx.dram("ag_o", [NTOK + 128, D], BF16)
    ag_it = ag_i.rearrange("(i p) d -> i p d", p=128)
    for i in range(NT):
        p.dma("sp", xtile[:], x_in_t[i], writes=[("x", 0)], desc=f"ldx{i}")
        layernorm_tile(ctx, lnt[:], xtile[:], lnscr, stat, i, src_key=("x", 0),
                       dst_key=("ln", 0))
        p.dma("pool", ag_it[i], lnt[:], reads=[("ln", 0)], writes=[("agi", i)],
              desc=f"stln{i}")
        for k in range(KD):
            transpose_128(ctx, ps_tr, "ps_tr", k, lnt[:, k * 128:(k + 1) * 128],
                          ident, reads=[("ln", 0)])
        dst = xTo[:, i * 1024:(i + 1) * 1024]
        p.op("dve", lambda E, dst=dst: E.tensor_copy(dst, ps_tr[:]),
             reads=["ps_tr"], writes=[("xTo", i)], desc=f"exTo{i}")
    p.op("pool", lambda E: E.collective_compute(
        "AllGather", ALU.bypass, [list(range(NCORES))], [ag_i.opt()],
        [ag_o[0:NTOK, :].opt()]),
        reads=[("agi", i) for i in range(NT)], writes=["ag_o"], desc="ag")

    xT_d = ctx.dram("xT_d", [KD * 128, NTOK], BF16)
    ag_ot = ag_o.rearrange("(i p) d -> i p d", p=128)
    for i in range(NGT):
        par = i % 2
        xsl = xts[:, par * 1024:(par + 1) * 1024]
        p.dma("sp", lnt[:], ag_ot[i], reads=["ag_o"],
              writes=["glt", ("ln", 0)], desc=f"ldg{i}")
        for k in range(KD):
            transpose_128(ctx, ps_tr, "ps_tr", k, lnt[:, k * 128:(k + 1) * 128],
                          ident, reads=["glt"])
        p.op("dve", lambda E, xsl=xsl: E.tensor_copy(xsl, ps_tr[:]),
             reads=["ps_tr"], writes=[("xts", par)], desc=f"ext{i}")
        dstx = xT_d.rearrange("(k p) t -> p k t", p=128)[:, :, i * 128:(i + 1) * 128]
        p.dma("pool", dstx, xsl.rearrange("p (k t) -> p k t", t=128),
              reads=[("xts", par)], writes=[("xTd", i)], desc=f"stxT{i}")
        mm_chain(ctx, ps0[:, 0:64], "ps0",
                 [(xsl[:, k * 128:(k + 1) * 128], [("xts", par)],
                   gwT[:, k * 64:(k + 1) * 64], ["gwT"]) for k in range(KD)],
                 desc=f"lg{i}")
        s_ = rsc[:, 0:64]
        p.op("act", lambda E, s_=s_: E.activation(out=s_, in_=ps0[:, 0:64],
                                                  func=AF.Sigmoid),
             reads=["ps0"], writes=[("rsc", "s")], desc=f"sg{i}")
        s4c = rsc[:, 64:128]
        p.op("dve", lambda E, s_=s_, s4c=s4c: E.tensor_add(s4c, s_, gbB[:]),
             reads=[("rsc", "s"), "gbB"], writes=[("rsc", "s4c")], desc=f"s4c{i}")
        gsc = rsc[:, 128:136]
        for g in range(8):
            p.op("dve", lambda E, g=g, s4c=s4c: E.max(out=rsc[:, 136:144],
                                                      in_=s4c[:, g * 8:(g + 1) * 8]),
                 reads=[("rsc", "s4c")], writes=[("rsc", "m8")], desc=f"gm{i}_{g}")
            p.op("dve", lambda E, g=g, gsc=gsc: E.tensor_add(
                gsc[:, g:g + 1], rsc[:, 136:137], rsc[:, 137:138]),
                reads=[("rsc", "m8")], writes=[("rsc", "gs")], desc=f"gs{i}_{g}")
        p.op("dve", lambda E, gsc=gsc: E.max(out=rsc[:, 136:144], in_=gsc),
             reads=[("rsc", "gs")], writes=[("rsc", "m8")], desc=f"g4{i}")
        gmk = rsc[:, 144:152]
        p.op("dve", lambda E, gsc=gsc, gmk=gmk: E.tensor_scalar(
            gmk, gsc, rsc[:, 139:140], None, ALU.is_ge),
            reads=[("rsc", "gs")], scalar_reads=[("rsc", "m8")],
            writes=[("rsc", "gmk")], desc=f"gmk{i}")
        msk = rsc[:, 64:128]
        for g in range(8):
            p.op("dve", lambda E, g=g, s4c=s4c: E.tensor_scalar(
                s4c[:, g * 8:(g + 1) * 8], s4c[:, g * 8:(g + 1) * 8],
                rsc[:, 144 + g:145 + g], None, ALU.mult),
                reads=[("rsc", "s4c")], scalar_reads=[("rsc", "gmk")],
                writes=[("rsc", "s4c")], desc=f"mex{i}_{g}")
        p.op("dve", lambda E, s4c=s4c: E.max(out=rsc[:, 136:144], in_=s4c),
             reads=[("rsc", "s4c")], writes=[("rsc", "m8")], desc=f"t8{i}")
        sel = rsc[:, 152:216]
        p.op("dve", lambda E, s4c=s4c, sel=sel: E.tensor_scalar(
            sel, s4c, rsc[:, 143:144], None, ALU.is_ge),
            reads=[("rsc", "s4c")], scalar_reads=[("rsc", "m8")],
            writes=[("rsc", "sel")], desc=f"sel{i}")
        p.op("dve", lambda E, sel=sel, s_=s_: E.tensor_mul(sel, sel, s_),
             reads=[("rsc", "sel"), ("rsc", "s")], writes=[("rsc", "sel")],
             desc=f"wn{i}")
        p.op("dve", lambda E, sel=sel: E.reduce_sum(rsc[:, 136:137], sel, axis=AX.X),
             reads=[("rsc", "sel")], writes=[("rsc", "den")], desc=f"den{i}")
        p.op("dve", lambda E: E.tensor_scalar(rsc[:, 136:137], rsc[:, 136:137],
                                              1e-20, None, ALU.add),
             reads=[("rsc", "den")], writes=[("rsc", "den")], desc=f"dene{i}")
        p.op("dve", lambda E: E.reciprocal(rsc[:, 137:138], rsc[:, 136:137]),
             reads=[("rsc", "den")], writes=[("rsc", "rden")], desc=f"rden{i}")
        p.op("dve", lambda E: E.tensor_scalar(rsc[:, 137:138], rsc[:, 137:138],
                                              2.5, None, ALU.mult),
             reads=[("rsc", "rden")], writes=[("rsc", "rden")], desc=f"rsc{i}")
        dst = coef[:, i * 8:(i + 1) * 8]
        p.op("dve", lambda E, dst=dst, sel=sel: E.tensor_scalar(
            dst, sel[:, 0:8], rsc[:, 137:138], None, ALU.mult),
            reads=[("rsc", "sel")], scalar_reads=[("rsc", "rden")],
            writes=[("coef", i)], desc=f"cf{i}")

    GCAP = 384
    HUGE = 1.0e9
    mgrp = ctx.sb("mgrp", [128, 64], F32)
    iot3 = ctx.sb("iot3", [128, 3 * 128], F32)
    tris = ctx.sb("tris", [128, 128], F32)
    tokc = ctx.sb("tokc", [128, 1], F32)
    dst8 = ctx.sb("dst8", [128, 8], F32)
    idxf = ctx.sb("idxf", [128, 24], F32)
    idxi = ctx.sb("idxi", [128, 384], I32)
    idxg = ctx.sb("idxg", [128, 3], I32)
    cfg = ctx.sb("cfg", [128, 8], F32)
    coef_d = ctx.dram("coef_d", [NTOK + 128, 8], F32)
    idxl_d = ctx.dram("idxl_d", [8, 16, GCAP], I32)
    for c3 in range(3):
        p.op("pool", lambda E, c3=c3: E.iota(iot3[:, c3 * 128:(c3 + 1) * 128],
                                             pattern=[[1, 128]], base=c3 * 128,
                                             channel_multiplier=0,
                                             allow_small_or_imprecise_dtypes=True),
             writes=["iot3"], desc=f"io{c3}")
    p.op("pool", lambda E: E.memset(tris[:], 1.0), writes=["tris"], desc="trz")
    p.op("pool", lambda E: E.affine_select(
        out=tris[:], in_=tris[:], compare_op=ALU.is_ge, fill=0.0,
        base=-1, pattern=[[1, 128]], channel_multiplier=-1),
        reads=["tris"], writes=["tris"], desc="tr1")
    p.op("pool", lambda E: E.iota(tokc[:], pattern=[[0, 1]], base=1,
                                  channel_multiplier=1,
                                  allow_small_or_imprecise_dtypes=True),
         writes=["tokc"], desc="tokc")
    dmpc = ctx.sb("dmpc", [128, 2], F32)
    p.op("dve", lambda E: E.tensor_scalar(dmpc[:, 0:1], tokc[:], float(NTOK - 1),
                                          None, ALU.add),
         reads=["tokc"], writes=["dmpc"], desc="dmp0")
    p.op("dve", lambda E: E.tensor_scalar(dmpc[:, 1:2], tokc[:], float(NTOK),
                                          None, ALU.add),
         reads=["tokc"], writes=["dmpc"], desc="dmp1")

    for i in range(NGT):
        j = i % 8
        dstm = mgrp.rearrange("p (e j) -> p e j", j=8)[:, :, j]
        p.op("dve", lambda E, dstm=dstm, i=i: E.tensor_scalar(
            dstm, coef[:, i * 8:(i + 1) * 8], 0.0, None, ALU.is_gt),
            reads=[("coef", i)], writes=[("mgrp", i // 8, j)], desc=f"mk{i}")
        p.dma("sp", coef_d[i * 128:(i + 1) * 128, :], coef[:, i * 8:(i + 1) * 8],
              reads=[("coef", i)], writes=[("coefd", i)], desc=f"stcf{i}")
        if j != 7:
            continue
        g = i // 8
        mg_keys = [("mgrp", g, jj) for jj in range(8)]
        for e in range(8):
            msl = mgrp[:, e * 8:(e + 1) * 8]
            rcum = rsc[:, 0:8]
            p.op("dve", lambda E, msl=msl, rcum=rcum: E.tensor_tensor_scan(
                rcum, onesf[:, 0:8], msl, 0.0, ALU.mult, ALU.add),
                reads=mg_keys + ["onesf"], writes=[("rsc", "rcum")],
                desc=f"rc{g}_{e}")
            p.op("pe", lambda E, rcum=rcum: E.matmul(
                ps0[:, 0:1], tris[:], rcum[:, 7:8], start=True, stop=True),
                reads=["tris", ("rsc", "rcum")], writes=["ps0"], desc=f"of{g}_{e}")
            p.op("dve", lambda E: E.tensor_copy(rsc[:, 8:9], ps0[:, 0:1]),
                 reads=["ps0"], writes=[("rsc", "offs")], desc=f"eo{g}_{e}")
            p.op("dve", lambda E, rcum=rcum: E.tensor_scalar(
                dst8[:], rcum, rsc[:, 8:9], -1.0, ALU.add, ALU.add),
                reads=[("rsc", "rcum")], scalar_reads=[("rsc", "offs")],
                writes=["dst8"], desc=f"d8{g}_{e}")
            p.op("dve", lambda E: E.tensor_scalar(dst8[:], dst8[:], dmpc[:, 0:1],
                                                  None, ALU.subtract),
                 reads=["dst8"], scalar_reads=["dmpc"], writes=["dst8"],
                 desc=f"dh{g}_{e}")
            p.op("dve", lambda E, msl=msl: E.tensor_mul(dst8[:], dst8[:], msl),
                 reads=["dst8"] + mg_keys, writes=["dst8"], desc=f"dm{g}_{e}")
            p.op("dve", lambda E: E.tensor_scalar(dst8[:], dst8[:], dmpc[:, 0:1],
                                                  None, ALU.add),
                 reads=["dst8"], scalar_reads=["dmpc"], writes=["dst8"],
                 desc=f"dp{g}_{e}")
            for j2 in range(8):
                dcol = dst8[:, j2:j2 + 1]
                tk = rsc[:, 9:10]
                p.op("dve", lambda E, tk=tk, g=g, j2=j2: E.tensor_scalar(
                    tk, tokc[:], float((g * 8 + j2) * 128), None, ALU.add),
                    reads=["tokc"], writes=[("rsc", "tk")], desc=f"tk{g}_{e}_{j2}")
                for c3 in range(3):
                    pb = rsc[:, 128 + c3 * 128:128 + (c3 + 1) * 128]
                    p.op("dve", lambda E, pb=pb, dcol=dcol, c3=c3: E.tensor_scalar(
                        pb, iot3[:, c3 * 128:(c3 + 1) * 128], dcol, None,
                        ALU.is_equal),
                        reads=["iot3"], scalar_reads=["dst8"],
                        writes=[("rsc", "pb", c3)], desc=f"pb{g}_{e}_{j2}_{c3}")
                    bank = (ps1, "ps1") if c3 == 0 else ((ps2, "ps2") if c3 == 1 else (ps3, "ps3"))
                    p.op("pe", lambda E, pb=pb, tk=tk, b=bank[0], e=e,
                         s=(j2 == 0), st=(j2 == 7), c3=c3: E.matmul(
                             b[0:128, e:e + 1] if c3 < 2 else b[0:128, e:e + 1],
                             pb, tk, start=s, stop=st),
                        reads=[("rsc", "pb", c3), ("rsc", "tk")],
                        writes=[bank[1]], desc=f"pm{g}_{e}_{j2}_{c3}")
        for c3, (bank, bkey) in enumerate(((ps1, "ps1"), (ps2, "ps2"), (ps3, "ps3"))):
            dstf = idxf[:, c3 * 8:(c3 + 1) * 8]
            p.op("dve", lambda E, dstf=dstf, bank=bank: E.tensor_scalar(
                dstf, bank[:, 0:8], -1.0, None, ALU.add),
                reads=[bkey], writes=[("idxf", c3)], desc=f"ie{g}_{c3}")
            p.op("dve", lambda E, dstf=dstf: E.tensor_scalar(
                rsc[:, 16:24], dstf, 0.0, None, ALU.is_lt),
                reads=[("idxf", c3)], writes=[("rsc", "neg")], desc=f"in{g}_{c3}")
            p.op("dve", lambda E, dstf=dstf: E.scalar_tensor_tensor(
                dstf, rsc[:, 16:24], dmpc[:, 1:2], dstf, ALU.mult, ALU.add),
                reads=[("rsc", "neg"), ("idxf", c3)], scalar_reads=["dmpc"],
                writes=[("idxf", c3)], desc=f"ih{g}_{c3}")
            p.op("pe", lambda E, dstf=dstf, c3=c3: E.transpose(
                ps4[0:8, c3 * 128:(c3 + 1) * 128], dstf, identf[:]),
                reads=[("idxf", c3), "identf"], writes=["ps4"], desc=f"it{g}_{c3}")
        p.op("dve", lambda E: E.tensor_copy(idxi[0:8, 0:384], ps4[0:8, 0:384]),
             reads=["ps4"], writes=["idxi"], desc=f"ic{g}")
        p.dma("sp", idxl_d[:, g, :], idxi[0:8, 0:384],
              reads=["idxi"], writes=[("idxl", g)], desc=f"sti{g}")

    yp_d = ctx.dram("yp_d", [NTOK + 128, D], F32)
    yp_t = yp_d.rearrange("(i p) d -> i p d", p=128)
    ypv = yp_d.rearrange("(a p) d -> p a d", p=128)

    wgscr2 = ctx.dram("wgscr2", [2 * 704, 1024], BF16)
    wuscr2 = ctx.dram("wuscr2", [2 * 704, 1024], BF16)
    wdscr2 = ctx.dram("wdscr2", [2 * 1024, 768], BF16)
    zpad = ctx.sb("zpad", [128, 512], BF16)
    p.op("pool", lambda E: E.memset(zpad[:], 0.0), writes=["zpad"], desc="zpad")
    for par in range(2):
        dstz = wdscr2[par * 1024:(par + 1) * 1024, 704:768].rearrange(
            "(k p) c -> p k c", p=128)
        p.dma("pool", dstz, zpad[:].rearrange("p (k c) -> p k c", c=64),
              reads=["zpad"], writes=[("wdz", par)], desc=f"wdz{par}")
    xg = ctx.sb("xg", [128, 2 * 1024], BF16)
    p.op("pool", lambda E: E.memset(xg[:], 0.0),
         writes=[("xg", 0), ("xg", 1)], desc="zxg")
    p.dma("pool", ag_o[NTOK:NTOK + 128, :], xg[:, 0:1024],
          reads=[("xg", 0)], writes=["agdump"], desc="zagd")
    p.dma("pool", coef_d[NTOK:NTOK + 128, :], xg[:, 0:8],
          reads=[("xg", 0)], writes=["cfdump"], desc="zcfd")
    for zi in range(64):
        p.dma("pool", ypv[:, zi * 2:(zi + 1) * 2, :],
              xg[:].rearrange("p (a d) -> p a d", a=2),
              reads=[("xg", 0), ("xg", 1)], writes=[("ypz", zi % 32)], desc=f"zy{zi}")
    ypz_keys = [("ypz", zi) for zi in range(32)]
    for e in range(8):
        par = e % 2
        p.dma("pool", wgscr2[par * 704:(par + 1) * 704, :], W["Wg"][e],
              writes=[("wgs", par)], desc=f"cwg{e}")
        p.dma("pool", wuscr2[par * 704:(par + 1) * 704, :], W["Wu"][e],
              writes=[("wus", par)], desc=f"cwu{e}")
        p.dma("pool", wdscr2[par * 1024:(par + 1) * 1024, 0:704], W["Wd"][e],
              writes=[("wds", par)], desc=f"cwd{e}")
        for k in range(KD):
            p.dma("sp", wgT[:, par * 5632 + k * 704:par * 5632 + (k + 1) * 704],
                  wgscr2[par * 704:(par + 1) * 704, k * 128:(k + 1) * 128],
                  reads=[("wgs", par)], writes=[("wgT", par)], desc=f"TWg{e}_{k}",
                  transpose=True)
            p.dma("sp", wuT[:, par * 5632 + k * 704:par * 5632 + (k + 1) * 704],
                  wuscr2[par * 704:(par + 1) * 704, k * 128:(k + 1) * 128],
                  reads=[("wus", par)], writes=[("wuT", par)], desc=f"TWu{e}_{k}",
                  transpose=True)
        for k in range(6):
            p.dma("sp", wdT[:, k * 1024:(k + 1) * 1024],
                  wdscr2[par * 1024:(par + 1) * 1024, k * 128:(k + 1) * 128],
                  reads=[("wds", par), ("wdz", par)], writes=[("wdT", 0)],
                  desc=f"TWd{e}_{k}", transpose=True)
        for g in range(16):
            p.dma("sp", idxg[:], idxl_d[e, g, :].rearrange("(c p) -> p c", p=128),
                  reads=[("idxl", g)], writes=["idxg"], desc=f"lix{e}_{g}",
                  allow_slow_non_contiguous=True)
            if dbg is not None and e == 0 and g == 0:
                p.dma("sp", dbg["idxg"], idxg[:], reads=["idxg"], desc="dbg_idxg")
            for st in range(3):
                xpar = st % 2
                xsl = xg[:, xpar * 1024:(xpar + 1) * 1024]
                p.op("pool", lambda E, xsl=xsl, st=st: E.indirect_dma_start(
                    out=xsl, out_offset=None, in_=ag_o,
                    in_offset=bass.IndirectOffsetOnAxis(ap=idxg[:, st:st + 1], axis=0)),
                    reads=["idxg", "ag_o", "agdump"], writes=[("xg", xpar)], dma=True,
                    desc=f"gx{e}_{g}_{st}")
                if dbg is not None and e == 0 and g == 0 and st == 0:
                    p.dma("sp", dbg["xg"], xsl, reads=[("xg", xpar)], desc="dbg_xg")
                p.op("pool", lambda E, st=st: E.indirect_dma_start(
                    out=cfg[:], out_offset=None, in_=coef_d,
                    in_offset=bass.IndirectOffsetOnAxis(ap=idxg[:, st:st + 1], axis=0)),
                    reads=["idxg", "cfdump"] + [("coefd", ii) for ii in range(NGT)],
                    writes=["cfg"], dma=True, desc=f"gc{e}_{g}_{st}")
                if dbg is not None and e == 0 and g == 0 and st == 0:
                    p.dma("sp", dbg["cfg"], cfg[:], reads=["cfg"], desc="dbg_cfg")
                for k in range(KD):
                    transpose_128(ctx, ps_tr, "ps_tr", k, xsl[:, k * 128:(k + 1) * 128],
                                  ident, reads=[("xg", xpar)])
                xts_sl = xts[:, xpar * 1024:(xpar + 1) * 1024]
                p.op("dve", lambda E, xts_sl=xts_sl: E.tensor_copy(xts_sl, ps_tr[:]),
                     reads=["ps_tr"], writes=[("xts", xpar)], desc=f"ext{e}_{g}_{st}")
                for half, w0, w1 in ((0, 0, 512), (1, 512, 704)):
                    bank, bkey = (ps0, "ps0") if half == 0 else (ps1, "ps1")
                    mm_chain(ctx, bank[:, 0:w1 - w0], bkey,
                             [(xts_sl[:, k * 128:(k + 1) * 128], [("xts", xpar)],
                               wgT[:, par * 5632 + k * 704 + w0:par * 5632 + k * 704 + w1],
                               [("wgT", par)]) for k in range(KD)],
                             desc=f"g{e}_{g}_{st}_{half}")
                    p.op("act", lambda E, bank=bank, n=w1 - w0: E.activation(
                        out=silu_s[:, 0:n], in_=bank[:, 0:n], func=AF.Silu),
                        reads=[bkey], writes=["silu_s"], desc=f"sl{e}_{g}_{st}_{half}")
                    bank2, bkey2 = (ps2, "ps2") if half == 0 else (ps3, "ps3")
                    mm_chain(ctx, bank2[:, 0:w1 - w0], bkey2,
                             [(xts_sl[:, k * 128:(k + 1) * 128], [("xts", xpar)],
                               wuT[:, par * 5632 + k * 704 + w0:par * 5632 + k * 704 + w1],
                               [("wuT", par)]) for k in range(KD)],
                             desc=f"u{e}_{g}_{st}_{half}")
                    hdst = htile[:, w0:w1]
                    p.op("dve", lambda E, hdst=hdst, bank2=bank2, n=w1 - w0: E.tensor_mul(
                        hdst, silu_s[:, 0:n], bank2[:, 0:n]),
                        reads=["silu_s", bkey2], writes=[("htile", half)],
                        desc=f"h{e}_{g}_{st}_{half}")
                for k in range(6):
                    transpose_128(ctx, ps_tr, "ps_tr", k, htile[:, k * 128:(k + 1) * 128],
                                  ident, reads=[("htile", 0), ("htile", 1), ("htile", "pad")])
                p.op("dve", lambda E: E.tensor_copy(hT[:, 0:768], ps_tr[:, 0:768]),
                     reads=["ps_tr"], writes=["hT"], desc=f"ehT{e}_{g}_{st}")
                for half in range(2):
                    bank, bkey = (ps4, "ps4") if half == 0 else (ps5, "ps5")
                    mm_chain(ctx, bank[:], bkey,
                             [(hT[:, k * 128:(k + 1) * 128], ["hT"],
                               wdT[:, k * 1024 + half * 512:k * 1024 + half * 512 + 512],
                               [("wdT", 0)]) for k in range(6)],
                             desc=f"d{e}_{g}_{st}_{half}")
                    dst = ysb[:, half * 512:(half + 1) * 512]
                    p.op("dve", lambda E, dst=dst, bank=bank, e=e: E.tensor_scalar(
                        dst, bank[:], cfg[:, e:e + 1], None, ALU.mult),
                        reads=[bkey], scalar_reads=["cfg"], writes=[("ysb", half)],
                        desc=f"ey{e}_{g}_{st}_{half}")
                if dbg is not None and e == 0 and g == 0 and st == 0:
                    p.dma("sp", dbg["ysb"], ysb[:], reads=[("ysb", 0), ("ysb", 1)],
                          desc="dbg_ysb")
                p.op("pool", lambda E, st=st: E.indirect_dma_start(
                    out=yp_d[0:128, :], out_offset=bass.IndirectOffsetOnAxis(
                        ap=idxg[:, st:st + 1], axis=0),
                    in_=ysb[:], in_offset=None, compute_op=ALU.add),
                    reads=["idxg", ("ysb", 0), ("ysb", 1), ("ypacc", e - 1)] + ypz_keys,
                    writes=[("ypacc", e), ("yp", 0)], dma=True,
                    desc=f"sy{e}_{g}_{st}")

    if dbg is not None:
        p.dma("sp", dbg["idxl"], idxl_d.rearrange("e g c -> e (g c)"),
              reads=[("idxl", g2) for g2 in range(16)], desc="dbg_idxl")
        p.dma("sp", dbg["coefd"], coef_d[0:NTOK, :], reads=[("coefd", ii) for ii in range(NGT)],
              desc="dbg_coefd")
        p.dma("sp", dbg["ypd"], yp_d[0:2048, :], reads=[("ypacc", 7), ("yp", 0)],
              desc="dbg_ypd")
    rs_d = ctx.dram("rs_d", [TLOC, D], F32)
    rs_t = rs_d.rearrange("(i p) d -> i p d", p=128)
    p.op("pool", lambda E: E.collective_compute(
        "ReduceScatter", ALU.add, [list(range(NCORES))], [yp_d[0:NTOK, :].opt()],
        [rs_d.opt()]),
        reads=[("yp", 0), ("ypacc", 7)], writes=["rs"], desc="rs")

    for i in range(NT):
        for nn, n0, n1 in ((0, 0, 512), (1, 512, 1024), (2, 1024, 1408)):
            bank, bkey = (ps0, "ps0") if nn % 2 == 0 else (ps1, "ps1")
            mm_chain(ctx, bank[:, 0:n1 - n0], bkey,
                     [(xTo[:, i * 1024 + k * 128:i * 1024 + (k + 1) * 128], [("xTo", i)],
                       sgT[:, k * 1408 + n0:k * 1408 + n1], ["sgT"]) for k in range(KD)],
                     desc=f"shg{i}_{nn}")
            p.op("act", lambda E, bank=bank, n=n1 - n0: E.activation(
                out=silu_s[:, 0:n], in_=bank[:, 0:n], func=AF.Silu),
                reads=[bkey], writes=["silu_s"], desc=f"shs{i}_{nn}")
            bank2, bkey2 = (ps2, "ps2") if nn % 2 == 0 else (ps3, "ps3")
            mm_chain(ctx, bank2[:, 0:n1 - n0], bkey2,
                     [(xTo[:, i * 1024 + k * 128:i * 1024 + (k + 1) * 128], [("xTo", i)],
                       suT[:, k * 1408 + n0:k * 1408 + n1], ["suT"]) for k in range(KD)],
                     desc=f"shu{i}_{nn}")
            hdst = htile[:, n0:n1]
            p.op("dve", lambda E, hdst=hdst, bank2=bank2, n=n1 - n0: E.tensor_mul(
                hdst, silu_s[:, 0:n], bank2[:, 0:n]),
                reads=["silu_s", bkey2], writes=[("htile", nn)], desc=f"shh{i}_{nn}")
        for k in range(11):
            transpose_128(ctx, ps_tr, "ps_tr", k % 8, htile[:, k * 128:(k + 1) * 128],
                          ident, reads=[("htile", 0), ("htile", 1), ("htile", 2)])
            if k % 8 == 7 or k == 10:
                n0 = (k // 8) * 8
                p.op("dve", lambda E, k=k, n0=n0: E.tensor_copy(
                    hT[:, n0 * 128:(k + 1) * 128], ps_tr[:, 0:((k % 8) + 1) * 128]),
                    reads=["ps_tr"], writes=["hT"], desc=f"ehTs{i}_{k}")
        for half in range(2):
            bank, bkey = (ps4, "ps4") if half == 0 else (ps5, "ps5")
            mm_chain(ctx, bank[:], bkey,
                     [(hT[:, k * 128:(k + 1) * 128], ["hT"],
                       sdT[:, k * 1024 + half * 512:k * 1024 + half * 512 + 512],
                       ["sdT"]) for k in range(11)],
                     desc=f"shd{i}_{half}")
        p.dma("sp", xtile[:], x_in_t[i], writes=[("x", 0)], desc=f"ldx2{i}")
        p.dma("sp", lnscr[:], rs_t[i], reads=["rs"], writes=["rsb"], desc=f"ldrs{i}")
        for half in range(2):
            bank, bkey = (ps4, "ps4") if half == 0 else (ps5, "ps5")
            dst = lnscr[:, half * 512:(half + 1) * 512]
            p.op("dve", lambda E, dst=dst, bank=bank: E.tensor_add(dst, dst, bank[:]),
                 reads=[bkey, "rsb"], writes=["rsb"], desc=f"cm{i}_{half}")
        p.op("dve", lambda E: E.scalar_tensor_tensor(
            xtile[:], lnscr[:], 0.5, xtile[:], ALU.mult, ALU.add),
            reads=["rsb", ("x", 0)], writes=[("x", 0)], desc=f"res{i}")
        p.dma("sp", x_out_t[i], xtile[:], reads=[("x", 0)], writes=[("xo", i)],
              desc=f"stx{i}")


_CTX_CACHE = {}


def _get_ctx(name, builder):
    if name not in _CTX_CACHE:
        ctx = Ctx(name)
        builder(ctx)
        ctx.emit()
        _CTX_CACHE[name] = ctx
    return _CTX_CACHE[name]


def _run(ctx, ims):
    ims = [{k: np.ascontiguousarray(np.asarray(v)) for k, v in im.items()}
           for im in ims]
    res = run_bass_kernel_spmd(ctx.nc, ims, core_ids=list(range(NCORES)))
    return res.results


def kernel(**inp):
    xf = np.ascontiguousarray(inp["L0_attn_x"].reshape(NTOK, D)).astype(np.float32)

    def shards(xf):
        return [xf[c * TLOC:(c + 1) * TLOC] for c in range(NCORES)]

    def batches(xf):
        return [xf[(c // 2) * T:(c // 2 + 1) * T] for c in range(NCORES)]

    def halo(xf):
        out = []
        for c in range(NCORES):
            b, h = c // 2, c % 2
            xh = np.zeros((TLOC + 128, D), np.float32)
            if h == 1:
                xh[:] = xf[b * T + TLOC - 128: b * T + 2 * TLOC]
            else:
                xh[128:] = xf[b * T: b * T + TLOC]
            out.append(xh)
        return out

    def asm(res):
        return np.concatenate([res[c]["x_out"] for c in range(NCORES)], axis=0)

    def b_l0ffn(ctx):
        xb = ctx.inp("xb_in", [T, D]); xo = ctx.inp("xown_in", [TLOC, D])
        xout = ctx.out("x_out", [TLOC, D])
        W = {"Wq": ctx.inp("Wq", [1024, 512]), "Wk": ctx.inp("Wk", [1024, 512]),
             "Ws": ctx.inp("Ws", [1024, 40]), "Wd": ctx.inp("Wd", [1024, 40]),
             "Wv": ctx.inp("Wv", [5, 1024, 64]), "Wo": ctx.inp("Wo", [5, 64, 1024])}
        build_L0_ffn(ctx, 0, xb, xo, xout, W)

    def b_l0attn(ctx):
        xb = ctx.inp("xb_in", [T, D]); xo = ctx.inp("xown_in", [TLOC, D])
        idx = ctx.inp("idxq_in", [TLOC, 1], I32); hc = ctx.inp("hcol_in", [128, 1])
        xout = ctx.out("x_out", [TLOC, D])
        W = {"wattn": ctx.inp("wattn", [3072, 1024]),
             "wproj": ctx.inp("wproj", [1024, 1024]),
             "loraA": ctx.inp("loraA", [1024, 64]), "loraB": ctx.inp("loraB", [64, 3072]),
             "loraA2": ctx.inp("loraA2", [1024, 64]), "loraB2": ctx.inp("loraB2", [64, 1024]),
             "gamma": ctx.inp("gamma", [3072]), "beta": ctx.inp("beta", [3072])}
        build_L0_attn(ctx, 0, xb, xo, xout, W, idx, hc)

    def b_l1ffn(ctx):
        xi = ctx.inp("x_in", [TLOC, D]); xout = ctx.out("x_out", [TLOC, D])
        W = {"w_router": ctx.inp("w_router", [1024]), "W1": ctx.inp("W1", [1024, 4096]),
             "b1": ctx.inp("b1", [4096]), "W2": ctx.inp("W2", [4096, 1024]),
             "b2": ctx.inp("b2", [1024])}
        build_L1_ffn(ctx, 0, xi, xout, W)

    def b_l1attn(ctx):
        xh = ctx.inp("xh_in", [TLOC + 128, D]); xo = ctx.inp("xown_in", [TLOC, D])
        hm = ctx.inp("hmask_in", [128, 1]); xout = ctx.out("x_out", [TLOC, D])
        W = {"W_in": ctx.inp("W_in", [4096, 1024]), "conv_w": ctx.inp("conv_w", [2048, 1, 4]),
             "conv_b": ctx.inp("conv_b", [2048]), "W_g": ctx.inp("W_g", [4096, 2048]),
             "b_g": ctx.inp("b_g", [4096]), "forget_base": ctx.inp("forget_base", [2048]),
             "W_out": ctx.inp("W_out", [1024, 2048])}
        build_L1_attn(ctx, 0, xh, xo, xout, W, hm)

    def b_l2ffn(ctx):
        xi = ctx.inp("x_in", [TLOC, D]); xout = ctx.out("x_out", [TLOC, D])
        W = {"router": ctx.inp("router", [1024, 8]),
             "eg": [ctx.inp(f"eg{e}", [1024, 512]) for e in range(8)],
             "eu": [ctx.inp(f"eu{e}", [1024, 512]) for e in range(8)],
             "ed": [ctx.inp(f"ed{e}", [512, 1024]) for e in range(8)],
             "sg": ctx.inp("sg", [1024, 1024]), "su": ctx.inp("su", [1024, 1024]),
             "sd": ctx.inp("sd", [1024, 1024])}
        build_L2_ffn(ctx, 0, xi, xout, W)

    def b_l2attn(ctx):
        xb = ctx.inp("xb_in", [T, D]); xo = ctx.inp("xown_in", [TLOC, D])
        xout = ctx.out("x_out", [TLOC, D])
        W = {"wq": ctx.inp("wq", [8, 1024, 64]), "bq": ctx.inp("bq", [8, 64]),
             "wk": ctx.inp("wk", [8, 1024, 64]), "bk": ctx.inp("bk", [8, 64]),
             "wv": ctx.inp("wv", [8, 1024, 64]), "bv": ctx.inp("bv", [8, 64]),
             "wo": ctx.inp("wo", [1024, 512]), "bo": ctx.inp("bo", [1024])}
        build_L2_attn(ctx, 0, xb, xo, xout, W)

    def b_l3ffn(ctx):
        xi = ctx.inp("x_in", [TLOC, D]); xout = ctx.out("x_out", [TLOC, D])
        W = {"gate_w": ctx.inp("gate_w", [64, 1024]), "gate_bias": ctx.inp("gate_bias", [64]),
             "Wg": ctx.inp("Wg", [8, 704, 1024]), "Wu": ctx.inp("Wu", [8, 704, 1024]),
             "Wd": ctx.inp("Wd", [8, 1024, 704]),
             "Sg": ctx.inp("Sg", [1408, 1024]), "Su": ctx.inp("Su", [1408, 1024]),
             "Sd": ctx.inp("Sd", [1024, 1408])}
        build_L3_ffn(ctx, 0, xi, xout, W)

    def b_l3attn(ctx):
        xb = ctx.inp("xb_in", [T, D]); xo = ctx.inp("xown_in", [TLOC, D])
        xout = ctx.out("x_out", [TLOC, D])
        W = {"wq": ctx.inp("wq", [1024, 512]), "wk": ctx.inp("wk", [1024, 512]),
             "wv": ctx.inp("wv", [1024, 512]), "wgq": ctx.inp("wgq", [1024, 512]),
             "wgk": ctx.inp("wgk", [1024, 512]), "wgv": ctx.inp("wgv", [1024, 512]),
             "r_out_w": ctx.inp("r_out_w", [512, 512]), "glw": ctx.inp("glw", [8, 64]),
             "grw": ctx.inp("grw", [8, 64]), "w_out": ctx.inp("w_out", [512, 1024])}
        build_L3_attn(ctx, 0, xb, xo, xout, W)

    def run_l0ffn(xf):
        ctx = _get_ctx("l0ffn", b_l0ffn)
        xb, xo = batches(xf), shards(xf)
        ims = [{"xb_in": xb[c], "xown_in": xo[c],
                "Wq": inp["L0_ffn_Wq"], "Wk": inp["L0_ffn_Wk"],
                "Ws": inp["L0_ffn_Ws"], "Wd": inp["L0_ffn_Wd"],
                "Wv": inp["L0_ffn_Wv"], "Wo": inp["L0_ffn_Wo"]} for c in range(NCORES)]
        return asm(_run(ctx, ims))

    def run_l0attn(xf):
        ctx = _get_ctx("l0attn", b_l0attn)
        xb, xo = batches(xf), shards(xf)
        ims = []
        for c in range(NCORES):
            h = c % 2
            H = h * TLOC
            ims.append({"xb_in": xb[c], "xown_in": xo[c],
                        "idxq_in": (np.arange(TLOC, dtype=np.int32) + H)[:, None],
                        "hcol_in": np.full((128, 1), float(H), np.float32),
                        "wattn": inp["L0_attn_W_attn"], "wproj": inp["L0_attn_W_proj"],
                        "loraA": inp["L0_attn_lora_attn_A"], "loraB": inp["L0_attn_lora_attn_B"],
                        "loraA2": inp["L0_attn_lora_proj_A"], "loraB2": inp["L0_attn_lora_proj_B"],
                        "gamma": inp["L0_attn_bn_gamma"], "beta": inp["L0_attn_bn_beta"]})
        return asm(_run(ctx, ims))

    def run_l1ffn(xf):
        ctx = _get_ctx("l1ffn", b_l1ffn)
        xo = shards(xf)
        ims = [{"x_in": xo[c], "w_router": inp["L1_ffn_w_router"],
                "W1": inp["L1_ffn_W1"], "b1": inp["L1_ffn_b1"],
                "W2": inp["L1_ffn_W2"], "b2": inp["L1_ffn_b2"]} for c in range(NCORES)]
        return asm(_run(ctx, ims))

    def run_l1attn(xf):
        ctx = _get_ctx("l1attn", b_l1attn)
        xh, xo = halo(xf), shards(xf)
        ims = [{"xh_in": xh[c], "xown_in": xo[c],
                "hmask_in": np.full((128, 1), float(c % 2), np.float32),
                "W_in": inp["L1_attn_W_in"], "conv_w": inp["L1_attn_conv_w"],
                "conv_b": inp["L1_attn_conv_b"], "W_g": inp["L1_attn_W_g"],
                "b_g": inp["L1_attn_b_g"], "forget_base": inp["L1_attn_forget_base"],
                "W_out": inp["L1_attn_W_out"]} for c in range(NCORES)]
        return asm(_run(ctx, ims))

    def run_l2ffn(xf):
        ctx = _get_ctx("l2ffn", b_l2ffn)
        xo = shards(xf)
        ims = []
        for c in range(NCORES):
            im = {"x_in": xo[c], "router": inp["L2_ffn_router"],
                  "sg": inp["L2_ffn_shared_gate"], "su": inp["L2_ffn_shared_up"],
                  "sd": inp["L2_ffn_shared_down"]}
            for e in range(8):
                im[f"eg{e}"] = inp["L2_ffn_experts_gate"][e]
                im[f"eu{e}"] = inp["L2_ffn_experts_up"][e]
                im[f"ed{e}"] = inp["L2_ffn_experts_down"][e]
            ims.append(im)
        return asm(_run(ctx, ims))

    def run_l2attn(xf):
        ctx = _get_ctx("l2attn", b_l2attn)
        xb, xo = batches(xf), shards(xf)
        ims = []
        for c in range(NCORES):
            h = c % 2
            sl = slice(h * 8, (h + 1) * 8)
            ims.append({"xb_in": xb[c], "xown_in": xo[c],
                        "wq": inp["L2_attn_Wq"][sl], "bq": inp["L2_attn_bq"][sl],
                        "wk": inp["L2_attn_Wk"][sl], "bk": inp["L2_attn_bk"][sl],
                        "wv": inp["L2_attn_Wv"][sl], "bv": inp["L2_attn_bv"][sl],
                        "wo": inp["L2_attn_Wo"][:, h * 512:(h + 1) * 512],
                        "bo": inp["L2_attn_bo"]})
        return asm(_run(ctx, ims))

    def run_l3ffn(xf):
        ctx = _get_ctx("l3ffn", b_l3ffn)
        xo = shards(xf)
        ims = []
        for c in range(NCORES):
            order = [c] + [g for g in range(8) if g != c]
            perm = np.concatenate([np.arange(8 * g, 8 * g + 8) for g in order])
            ims.append({"x_in": xo[c],
                        "gate_w": np.ascontiguousarray(inp["L3_ffn_gate_w"][perm]),
                        "gate_bias": np.ascontiguousarray(inp["L3_ffn_gate_bias"][perm]),
                        "Wg": inp["L3_ffn_Wg"][8 * c:8 * c + 8],
                        "Wu": inp["L3_ffn_Wu"][8 * c:8 * c + 8],
                        "Wd": inp["L3_ffn_Wd"][8 * c:8 * c + 8],
                        "Sg": inp["L3_ffn_Sg"], "Su": inp["L3_ffn_Su"],
                        "Sd": inp["L3_ffn_Sd"]})
        return asm(_run(ctx, ims))

    def run_l3attn(xf):
        ctx = _get_ctx("l3attn", b_l3attn)
        xb, xo = batches(xf), shards(xf)
        ims = []
        for c in range(NCORES):
            h = c % 2
            hsl = slice(h * 512, (h + 1) * 512)
            ims.append({"xb_in": xb[c], "xown_in": xo[c],
                        "wq": inp["L3_attn_wq"][:, hsl], "wk": inp["L3_attn_wk"][:, hsl],
                        "wv": inp["L3_attn_wv"][:, hsl],
                        "wgq": inp["L3_attn_wgq"], "wgk": inp["L3_attn_wgk"],
                        "wgv": inp["L3_attn_wgv"],
                        "r_out_w": inp["L3_attn_r_out_w"][:, hsl],
                        "glw": inp["L3_attn_gate_local_w"][h * 8:(h + 1) * 8],
                        "grw": inp["L3_attn_gate_remote_w"][h * 8:(h + 1) * 8],
                        "w_out": inp["L3_attn_w_out"][hsl, :]})
        return asm(_run(ctx, ims))

    xf = run_l0ffn(xf)
    xf = run_l0attn(xf)
    xf = run_l0ffn(xf)
    xf = run_l1ffn(xf)
    xf = run_l1attn(xf)
    xf = run_l1ffn(xf)
    xf = run_l2ffn(xf)
    xf = run_l2attn(xf)
    xf = run_l2ffn(xf)
    xf = run_l3ffn(xf)
    xf = run_l3attn(xf)
    xf = run_l3ffn(xf)
    return xf.reshape(B, T, D).astype(np.float32)
